# version 20
# baseline (speedup 1.0000x reference)
import contextlib
import numpy as np
import ml_dtypes
import concourse.bass as bass
import concourse.mybir as mybir
from concourse.bass_utils import run_bass_kernel_spmd

F32 = mybir.dt.float32
BF16 = mybir.dt.bfloat16
U8 = mybir.dt.uint8
U32 = mybir.dt.uint32
I32 = mybir.dt.int32
AF = mybir.ActivationFunctionType
ALU = mybir.AluOpType

HID = 4096
INTER = 11008
GROUP = 128
NCORES = 8
F_PER = INTER // NCORES
BETA_PER = F_PER // 2
N_DG = 86
SEQ = 2048
NB = 8
BS = 256
NBATCH = 4
TOK = 8192
TOK_PER = TOK // NCORES

TILES = []
_off = 0
for _t5 in range(6):
    _K = 128 if _t5 < 5 else 48
    for _par in (0, 1):
        TILES.append((_t5, _par, _K, _off))
        _off += _K
assert _off == F_PER
BLOCKS = [TILES[0:6], TILES[6:12]]

PERM_HID = np.arange(HID).reshape(16, 128, 2).transpose(0, 2, 1).reshape(-1)


def f_local(t5, par, pp):
    return 256 * t5 + 2 * pp + par


def chunks(total, step):
    out = []
    o = 0
    while o < total:
        out.append((o, min(step, total - o)))
        o += step
    return out


def decay_tables(head_global):
    base = 1.0 / 2.0 ** (8.0 / 32)
    factor = 1.0 - 1.0 / (31 + 1e-5) + 1e-5
    slope = (base ** (head_global + 1)) * factor
    n = np.arange(BS)
    qdec = np.exp(-slope * (n + 1))
    kdec = np.exp(-slope * (BS - (n + 1)))
    dd = np.where(n[None, :] >= n[:, None], np.exp(-slope * (n[None, :] - n[:, None])), 0.0)
    bdec = float(np.exp(-slope * BS))
    return qdec, kdec, dd, bdec


class Prog:
    ENGINES = ("sync", "tensor", "vector", "scalar", "gpsimd")

    def __init__(self, nc):
        self.nc = nc
        self.events = {e: [] for e in self.ENGINES}
        self.counts = {}
        self.handles = {}
        self.gate = []

    def sem(self, stack, name):
        self.handles[name] = stack.enter_context(self.nc.semaphore(name))
        self.counts[name] = 0

    def set_gate(self, waits):
        self.gate = list(waits)

    def emit(self, eng, fn, wait=(), inc=None, inc_n=1):
        self.events[eng].append((self.gate + list(wait), fn, inc, inc_n))
        if inc is not None:
            self.counts[inc] += inc_n
        return self.counts.get(inc)

    def dma(self, eng, out, in_, inc, wait=()):
        return self.emit(eng, lambda e: e.dma_start(out=out, in_=in_), wait, inc, 16)

    def c(self, sem):
        return self.counts[sem]

    def snap(self, names):
        return [(s, self.counts[s]) for s in names]

    def replay(self, block):
        for eng_name in self.ENGINES:
            evs = self.events[eng_name]
            if not evs:
                continue

            def make(evs):
                def body(eng):
                    lastw = {}
                    for waits, fn, inc, inc_n in evs:
                        for s, cval in waits:
                            if cval <= 0 or lastw.get(s, -1) >= cval:
                                continue
                            eng.wait_ge(self.handles[s], cval)
                            lastw[s] = cval
                        inst = fn(eng)
                        if inc is not None:
                            inst.then_inc(self.handles[inc], inc_n)
                return body

            getattr(block, eng_name)(make(evs))


def emit_phase0(p, t):
    nc = p.nc
    p.dma("sync", t["ident_sb"][:], t["ident"], "ld")
    ident_c = p.c("ld")
    p.emit("vector", lambda e: e.memset(t["stg_sb"][:], 0.0), [], "p0v")
    z_c = p.c("p0v")
    for blk, half in ((10, 0), (10, 1), (11, 0), (11, 1)):
        p.dma("gpsimd", t["hT_dram"][128 * blk + 48:128 * blk + 128, 4096 * half:4096 * half + 4096],
              t["stg_sb"][0:80, :], "agi", [("p0v", z_c)])

    for ti in range(8):
        p.dma("sync", t["x_sb"][ti % 2][:], t["x_own"][128 * ti:128 * ti + 128, :], "xld",
              [("p0s", max(0, p.c("p0s") - 1))])
        xld_c = p.c("xld")
        p.emit("scalar", lambda e, ti=ti: e.activation(
            t["sq_scr"][:], t["x_sb"][ti % 2][:], AF.Square, accum_out=t["ssq"][:]),
            [("xld", xld_c), ("p0v", max(0, p.c("p0v") - 4))], "p0s")
        sq_c = p.c("p0s")
        p.emit("vector", lambda e, ti=ti: e.reduce_sum(
            t["mean_c"][:], t["x_sb"][ti % 2][:], mybir.AxisListType.X),
            [("xld", xld_c)], "p0v")
        p.emit("vector", lambda e: e.tensor_scalar_mul(t["mean_c"][:], t["mean_c"][:], 1.0 / HID),
               [("p0v", p.c("p0v"))], "p0v")
        p.emit("vector", lambda e: e.tensor_tensor(t["msq_c"][:], t["mean_c"][:], t["mean_c"][:], ALU.mult),
               [("p0v", p.c("p0v"))], "p0v")
        p.emit("vector", lambda e: e.scalar_tensor_tensor(
            t["var_c"][:], t["ssq"][:], 1.0 / HID, t["msq_c"][:], ALU.mult, ALU.subtract),
            [("p0s", sq_c), ("p0v", p.c("p0v"))], "p0v")
        v_c = p.c("p0v")
        p.emit("scalar", lambda e: e.activation(t["std_c"][:], t["var_c"][:], AF.Sqrt, bias=np.float32(1e-5)),
               [("p0v", v_c), ("p0s", p.c("p0s"))], "p0s")
        s_c = p.c("p0s")
        p.emit("vector", lambda e: e.reciprocal(t["rstd_c"][:], t["std_c"][:]),
               [("p0s", s_c), ("p0v", p.c("p0v"))], "p0v")
        p.emit("vector", lambda e: e.scalar_tensor_tensor(
            t["nmr_c"][:], t["mean_c"][:], -1.0, t["rstd_c"][:], ALU.mult, ALU.mult),
            [("p0v", p.c("p0v"))], "p0v")
        r_c = p.c("p0v")
        p.emit("scalar", lambda e, ti=ti: e.activation(
            t["xn_sb"][:], t["x_sb"][ti % 2][:], AF.Identity, bias=t["nmr_c"][:], scale=t["rstd_c"][:]),
            [("p0v", r_c), ("p0t", p.c("p0t"))], "p0s")
        xn_c = p.c("p0s")
        for grp in range(4):
            bank = grp % 2
            wtr = [("p0s", xn_c), ("ld", ident_c), ("p0e", max(0, p.c("p0e") - 1))]
            for j in range(8):
                kk = grp * 8 + j
                dg, par = kk // 2, kk % 2
                p.emit("tensor", lambda e, dg=dg, par=par, j=j, bank=bank: e.transpose(
                    t["ptr0"][bank][:, 128 * j:128 * j + 128],
                    t["xn_sb"][:, 256 * dg + par:256 * dg + 256:2],
                    t["ident_sb"][:]), wtr if j == 0 else [], "p0t" if j == 7 else None)
            tr_c = p.c("p0t")
            p.emit("vector", lambda e, grp=grp, bank=bank: e.tensor_copy(
                t["stg_sb"][:, 1024 * grp:1024 * grp + 1024], t["ptr0"][bank][:]),
                [("p0t", tr_c), ("agi", p.c("agi"))], "p0e")
        e_c = p.c("p0e")
        p.dma("gpsimd",
              t["ag1_in"][:, 128 * ti:128 * ti + 128].rearrange("(k p) t -> p k t", p=128),
              t["stg_sb"][:].rearrange("p (k t) -> p k t", t=128), "agi", [("p0e", e_c)])
    p.emit("gpsimd", lambda e: e.collective_compute(
        "AllGather", ALU.bypass, replica_groups=[list(range(NCORES))],
        ins=[t["ag1_in_h"].ap().opt()], outs=[t["ag1_out_h"].ap().opt()]),
        [("agi", p.c("agi"))], "cc1")



def emit_ffn_gu(p, t):
    nc = p.nc
    p.dma("sync", t["sc_sb"][:], t["sc_gu"], "ld")
    p.dma("sync", t["qzt"][:], t["qz_gu"], "ld")
    p.dma("sync", t["indf"][:], t["ind_hid_f32"], "ld")
    p.dma("sync", t["indb"][:], t["ind_hid_b"], "ld")
    ld_all = p.c("ld")
    wld = [("ld", ld_all)]
    p.emit("vector", lambda e: e.tensor_single_scalar(
        t["zq8"][:, 0:1376].bitcast(U32), t["qzt"][:].bitcast(U32), 0x0F0F0F0F, ALU.bitwise_and), wld, "prep")
    p.emit("vector", lambda e: e.tensor_scalar(
        t["zq8"][:, 1376:2752].bitcast(U32), t["qzt"][:].bitcast(U32), 4, 0x0F0F0F0F,
        ALU.logical_shift_right, ALU.bitwise_and), wld, "prep")
    zq_done = p.c("prep")
    for half, nscz in ((0, "nscz_g"), (1, "nscz_u")):
        for (t5, par, K, off) in TILES:
            src = t["zq8"][:, 1376 * par + 688 * half + 128 * t5:
                           1376 * par + 688 * half + 128 * t5 + K]
            dst = t[nscz][:, off:off + K]
            in1 = t["sc_sb"][:, 1376 * half + off:1376 * half + off + K]
            p.emit("vector", lambda e, s=src, d=dst, i1=in1: e.scalar_tensor_tensor(
                d, s, 1.0, i1, ALU.add, ALU.mult), [("prep", zq_done)], "prep")
    nscz_pos = p.c("prep")
    p.emit("vector", lambda e: e.tensor_scalar_mul(t["nscz_g"][:], t["nscz_g"][:], -1.0), [("prep", nscz_pos)], "prep")
    p.emit("vector", lambda e: e.tensor_scalar_mul(t["nscz_u"][:], t["nscz_u"][:], -1.0), [("prep", nscz_pos)], "prep")

    hts_seq = 0
    for bi, btiles in enumerate(BLOCKS):
        bo = btiles[0][3]
        bW = sum(K for (_, _, K, _) in btiles)
        deq_at_dg = []
        for dg in range(16):
            db = dg % 2
            deq_at_dg.append(p.c("deq"))
            if dg >= 2:
                wq = [("deq", deq_at_dg[dg - 1])]
            elif bi > 0:
                wq = [("deq", p.c("deq"))]
            else:
                wq = []
            qsem = "qb" + str(db)
            p.dma("sync", t["qbs"][db][:, 0:bW], t["qb_gu"][128 * dg:128 * dg + 128, bo:bo + bW], qsem, wq)
            p.dma("sync", t["qbs"][db][:, bW:2 * bW], t["qb_gu"][128 * dg:128 * dg + 128, 1376 + bo:1376 + bo + bW], qsem, wq)
            qb_c = p.c(qsem)
            wnib = [(qsem, qb_c)]
            if dg >= 2:
                wnib.append(("deq", deq_at_dg[dg - 1]))
            elif bi > 0:
                wnib.append(("deq", p.c("deq")))
            if bi > 0 and dg == 0:
                wnib.append(("gu", p.c("gu")))
            p.emit("vector", lambda e, bW=bW, db=db: e.tensor_single_scalar(
                t["nibs0"][db][:].bitcast(U32)[:, 0:bW // 2], t["qbs"][db][:].bitcast(U32)[:, 0:bW // 2],
                0x0F0F0F0F, ALU.bitwise_and), wnib, "nib")
            p.emit("vector", lambda e, bW=bW, db=db: e.tensor_scalar(
                t["nibs1"][db][:].bitcast(U32)[:, 0:bW // 2], t["qbs"][db][:].bitcast(U32)[:, 0:bW // 2],
                4, 0x0F0F0F0F, ALU.logical_shift_right, ALU.bitwise_and), wnib, "nib")
            nib_c = p.c("nib")
            for half in (0, 1):
                for (co, cw) in chunks(bW, 512):
                    bank = (half * 2 + (co // 512)) % 2
                    wsc = [("ld", ld_all), ("deq", max(0, p.c("deq") - 2))]
                    sc_rhs = t["sc_sb"][:, 1376 * half + bo + co: 1376 * half + bo + co + cw]
                    ind_l = t["indf"][:, 128 * (2 * dg):128 * (2 * dg) + 128]
                    p.emit("tensor", lambda e, o=t["psc"][bank], il=ind_l, r=sc_rhs, cw=cw:
                           e.matmul(o[:, 0:cw], il, r, start=True, stop=True), wsc, "sc")
                    sc_c = p.c("sc")
                    for par in (0, 1):
                        kk = 2 * dg + par
                        nib = t["nibs0"][db] if par == 0 else t["nibs1"][db]
                        dst = t["w_sb"][:, kk * 2 * bW + half * bW + co: kk * 2 * bW + half * bW + co + cw]
                        wmult = [("nib", nib_c), ("sc", sc_c)]
                        if bi > 0 and dg == 0:
                            wmult.append(("gu", p.c("gu")))
                        p.emit("vector", lambda e, d=dst, n=nib, co2=half * bW + co, cw=cw, b=bank:
                               e.tensor_tensor(d, n[:, co2:co2 + cw], t["psc"][b][:, 0:cw], ALU.mult),
                               wmult, "deq")
        deq_block = p.c("deq")

        for T in range(16):
            r, coff = (512 * T) // 1024, (512 * T) % 1024
            wx = [("cc1", 1)] + ([("gu", p.c("gu"))] if (T > 0 or bi > 0) else [])
            p.dma("sync", t["xt"][:].rearrange("p (k t) -> p k t", t=512),
                  t["ag1_out"][4096 * r:4096 * r + 4096, coff:coff + 512]
                  .rearrange("(k p) t -> p k t", p=128), "x", wx)
            x_c = p.c("x")
            wsg = [("x", x_c), ("sgc", p.c("sgc"))]
            for kk in range(32):
                p.emit("tensor", lambda e, kk=kk: e.matmul(
                    t["psg"][:], t["indb"][:, 32 * kk:32 * kk + 32], t["xt"][:, 512 * kk:512 * kk + 512],
                    start=(kk == 0), stop=(kk == 31)),
                    wsg if kk == 0 else [], "sg" if kk == 31 else None)
            sg_c = p.c("sg")
            p.emit("vector", lambda e: e.tensor_copy(t["sg_sb"][:], t["psg"][:]), [("sg", sg_c)], "sgc")
            sgc_c = p.c("sgc")

            for ji, (t5, par, K, off) in enumerate(btiles):
                pr = ji % 2
                pg, pu = t["pg"][pr], t["pu"][pr]
                loff = off - bo
                wg = [("deq", deq_block), ("sgc", sgc_c), ("sil", max(0, p.c("sil") - 1)), ("x", x_c)]
                for kk in range(32):
                    lhs = t["w_sb"][:, kk * 2 * bW + loff:kk * 2 * bW + loff + K]
                    p.emit("tensor", lambda e, l=lhs, kk=kk, pg=pg, K=K: e.matmul(
                        pg[0:K, :], l, t["xt"][:, 512 * kk:512 * kk + 512],
                        start=(kk == 0), stop=False), wg if kk == 0 else [], None)
                p.emit("tensor", lambda e, K=K, off=off, pg=pg: e.matmul(
                    pg[0:K, :], t["nscz_g"][:, off:off + K], t["sg_sb"][:],
                    start=False, stop=True), [], "gu")
                g_c = p.c("gu")
                wu = [("hT", max(0, p.c("hT") - 1))]
                for kk in range(32):
                    lhs = t["w_sb"][:, kk * 2 * bW + bW + loff:kk * 2 * bW + bW + loff + K]
                    p.emit("tensor", lambda e, l=lhs, kk=kk, pu=pu, K=K: e.matmul(
                        pu[0:K, :], l, t["xt"][:, 512 * kk:512 * kk + 512],
                        start=(kk == 0), stop=False), wu if kk == 0 else [], None)
                p.emit("tensor", lambda e, K=K, off=off, pu=pu: e.matmul(
                    pu[0:K, :], t["nscz_u"][:, off:off + K], t["sg_sb"][:],
                    start=False, stop=True), [], "gu")
                u_c = p.c("gu")
                sb_ = hts_seq % 2
                p.emit("scalar", lambda e, K=K, pg=pg, sb_=sb_: e.activation(
                    t["silu_t"][sb_][0:K, :], pg[0:K, :], AF.Silu),
                    [("gu", g_c), ("hT", max(0, p.c("hT") - 1))], "sil")
                sil_c = p.c("sil")
                hb = hts_seq % 2
                hsem = "hts" + str(hb)
                whT = [("sil", sil_c), ("gu", u_c), (hsem, p.c(hsem))]
                p.emit("vector", lambda e, K=K, hb=hb, pu=pu, sb_=sb_: e.tensor_tensor(
                    t["hT_sb"][hb][0:K, :], t["silu_t"][sb_][0:K, :], pu[0:K, :], ALU.mult),
                    whT, "hT")
                hT_c = p.c("hT")
                jtg = bi * 6 + ji
                p.dma("gpsimd", t["hT_dram"][128 * jtg:128 * jtg + K, 512 * T:512 * T + 512],
                      t["hT_sb"][hb][0:K, :], hsem, [("hT", hT_c)])
                hts_seq += 1
    return p.c("gu"), (p.c("hts0"), p.c("hts1"))


def emit_ffn_down(p, t, gu_all, hts_all):
    nc = p.nc
    for q in range(4):
        p.dma("gpsimd", t["qkv_bf"][1024 * q:1024 * q + 1024, :],
              t["qkv_wT"][1024 * q:1024 * q + 1024, :], "wcast")
    wph = [("gu", p.c("gu")), ("hT", p.c("hT")), ("sil", p.c("sil"))]
    p.dma("sync", t["scd"][:], t["sc_down"], "ld", wph)
    p.dma("sync", t["qzd"][:], t["qz_down"], "ld", wph)
    p.dma("sync", t["inddf"][:], t["ind_dn_f32"], "ld", wph)
    p.dma("sync", t["inddb"][:], t["ind_dn_b"], "ld", wph)
    ld_all = p.c("ld")
    wld = [("ld", ld_all)]
    p.emit("vector", lambda e: e.tensor_single_scalar(
        t["zqd8"][:, 0:2048].bitcast(U32), t["qzd"][:].bitcast(U32), 0x0F0F0F0F, ALU.bitwise_and), wld, "prep")
    p.emit("vector", lambda e: e.tensor_scalar(
        t["zqd8"][:, 2048:4096].bitcast(U32), t["qzd"][:].bitcast(U32), 4, 0x0F0F0F0F,
        ALU.logical_shift_right, ALU.bitwise_and), wld, "prep")
    zqd_done = p.c("prep")
    for par in (0, 1):
        src = t["zqd8"][:, 2048 * par:2048 * par + 2048]
        dst = t["nsczd"][:, par::2]
        in1 = t["scd"][:, par::2]
        p.emit("vector", lambda e, s=src, d=dst, i1=in1: e.scalar_tensor_tensor(
            d, s, 1.0, i1, ALU.add, ALU.mult), [("prep", zqd_done)], "prep")
    nsczd_pos = p.c("prep")
    p.emit("vector", lambda e: e.tensor_scalar_mul(t["nsczd"][:], t["nsczd"][:], -1.0), [("prep", nsczd_pos)], "prep")
    prep_all = p.c("prep")

    for t5 in range(6):
        K = 128 if t5 < 5 else 48
        wq = [("deqd", p.c("deqd"))] if t5 > 0 else [("gu", gu_all)]
        p.dma("sync", t["qbd_sb"][0:K, :], t["qb_down"][128 * t5:128 * t5 + K, :], "qbd", wq)
        qbd_c = p.c("qbd")
        p.emit("vector", lambda e, K=K: e.tensor_single_scalar(
            t["nibd0"][0:K, :].bitcast(U32), t["qbd_sb"][0:K, :].bitcast(U32),
            0x0F0F0F0F, ALU.bitwise_and), [("qbd", qbd_c)], "nibd")
        p.emit("vector", lambda e, K=K: e.tensor_scalar(
            t["nibd1"][0:K, :].bitcast(U32), t["qbd_sb"][0:K, :].bitcast(U32),
            4, 0x0F0F0F0F, ALU.logical_shift_right, ALU.bitwise_and), [("qbd", qbd_c)], "nibd")
        nibd_c = p.c("nibd")
        for par in (0, 1):
            kt = 2 * t5 + par
            for (co, cw) in chunks(4096, 512):
                bank = (co // 512) % 2
                wsc = [("ld", ld_all), ("deqd", p.c("deqd"))]
                p.emit("tensor", lambda e, b=bank, kt=kt, co=co, cw=cw, K=K: e.matmul(
                    t["psc"][b][0:K, 0:cw], t["inddf"][:, TILES[kt][3]:TILES[kt][3] + K],
                    t["scd"][:, co:co + cw], start=True, stop=True), wsc, "scd_")
                scd_c = p.c("scd_")
                nib = t["nibd0"] if par == 0 else t["nibd1"]
                p.emit("vector", lambda e, kt=kt, co=co, cw=cw, K=K, n=nib, b=bank:
                       e.tensor_tensor(t["wd_sb"][0:K, 4096 * kt + co:4096 * kt + co + cw],
                                       n[0:K, co:co + cw], t["psc"][b][0:K, 0:cw], ALU.mult),
                       [("nibd", nibd_c), ("scd_", scd_c)], "deqd")
    deqd_all = p.c("deqd")

    rs_done = []
    for k4 in range(4):
      for cp in range(16):
        ti = 8 * (cp // 2) + 2 * k4 + (cp % 2)
        hb = ti % 2
        whts = [("hts0", hts_all[0]), ("hts1", hts_all[1])]
        if ti == 0:
            wht = whts
        else:
            wht = whts + [("pd", max(0, p.c("pd") - 8))]
        hlsem = "htl" + str(hb)
        p.dma("sync", t["ht"][hb][:].rearrange("p (k t) -> p k t", t=128),
              t["hT_dram"][:, 128 * ti:128 * ti + 128].rearrange("(k p) t -> p k t", p=128),
              hlsem, wht)
        htl_c = p.c(hlsem)
        wsh = [(hlsem, htl_c), ("shc", p.c("shc"))]
        for kt in range(12):
            K = TILES[kt][2]
            p.emit("tensor", lambda e, kt=kt, K=K, hb=hb: e.matmul(
                t["psh"][:, 0:128], t["inddb"][0:K, 12 * kt:12 * kt + 12],
                t["ht"][hb][0:K, 128 * kt:128 * kt + 128],
                start=(kt == 0), stop=(kt == 11)), wsh if kt == 0 else [], "sh" if kt == 11 else None)
        sh_c = p.c("sh")
        p.emit("vector", lambda e: e.tensor_copy(t["sh_sb"][:], t["psh"][:, 0:128]), [("sh", sh_c)], "shc")
        shc_c = p.c("shc")
        rb = ti % 2
        for ho in range(8):
            bank = ho % 4
            pd = t["pd_"][bank]
            wd0 = [("deqd", deqd_all), ("prep", prep_all), ("shc", shc_c),
                   ("rsv", max(0, p.c("rsv") - 3))]
            for kt in range(12):
                K = TILES[kt][2]
                p.emit("tensor", lambda e, kt=kt, K=K, ho=ho, pd=pd, hb=hb: e.matmul(
                    pd[:], t["ht"][hb][0:K, 128 * kt:128 * kt + 128],
                    t["wd_sb"][0:K, 4096 * kt + 512 * ho:4096 * kt + 512 * ho + 512],
                    start=(kt == 0), stop=False), wd0 if kt == 0 else [], None)
            p.emit("tensor", lambda e, ho=ho, pd=pd: e.matmul(
                pd[:], t["sh_sb"][:], t["nsczd"][:, 512 * ho:512 * ho + 512],
                start=False, stop=True), [], "pd")
            pd_c = p.c("pd")
            rsem = "rss" + str(rb)
            wev = [("pd", pd_c), (rsem, p.c(rsem))]
            p.emit("vector", lambda e, rb=rb, ho=ho, pd=pd: e.tensor_copy(
                t["rs_sb"][rb][:, 512 * ho:512 * ho + 512], pd[:]), wev, "rsv")
            rsv_c = p.c("rsv")
            if ho == 7:
                rrow = 2048 * k4 + 256 * (ti // 8) + 128 * (ti % 2)
                p.dma("scalar", t["rs_in"][rrow:rrow + 128, :], t["rs_sb"][rb][:],
                      rsem, [("rsv", rsv_c)])
      p.emit("gpsimd", lambda e, k4=k4: e.collective_compute(
          "ReduceScatter", ALU.add, replica_groups=[list(range(NCORES))],
          ins=[t["rs_in"][2048 * k4:2048 * k4 + 2048, :]],
          outs=[t["rs_out"][256 * k4:256 * k4 + 256, :]]),
          [("rss0", p.c("rss0")), ("rss1", p.c("rss1"))], "cc2")


def emit_attn(p, t):
    nc = p.nc
    p.dma("sync", t["qdec_sb"][:], t["qdec"], "ald")
    p.dma("sync", t["kdec_sb"][:], t["kdec"], "ald")
    p.dma("sync", t["ddT_sb"][:], t["ddT"], "ald")
    ald_all = p.c("ald")
    ident_c = 16

    p.dma("sync", t["qkvw"][:].rearrange("p (k o) -> p k o", o=1536),
          t["qkv_bf"][:, :].rearrange("(k p) o -> p k o", p=128),
          "qwl", [("wcast", p.c("wcast"))])
    qwl_c = p.c("qwl")

    for b in range(NBATCH):
        for T in range(SEQ // 512):
            tok0 = 2048 * b + 512 * T
            r, coff = tok0 // 1024, tok0 % 1024
            ab = T % 2
            wx = [("cc1", 1), ("pqkv", p.c("pqkv"))]
            p.dma("sync", t["axt"][ab][:].rearrange("p (k t) -> p k t", t=512),
                  t["ag1_out"][4096 * r:4096 * r + 4096, coff:coff + 512]
                  .rearrange("(k p) t -> p k t", p=128), "ax", wx)
            ax_c = p.c("ax")
            for o in range(12):
                bank = o % 2
                wmm = [("ax", ax_c), ("qwl", qwl_c), ("sqkv", max(0, p.c("sqkv") - 1))]
                if o < 2:
                    wmm = wmm + [("am20", p.c("am20")), ("am21", p.c("am21"))]
                for k in range(32):
                    p.emit("tensor", lambda e, o=o, k=k, bank=bank, ab=ab: e.matmul(
                        t["pqkv"][bank][:], t["qkvw"][:, 1536 * k + 128 * o:1536 * k + 128 * o + 128],
                        t["axt"][ab][:, 512 * k:512 * k + 512],
                        start=(k == 0), stop=(k == 31)),
                        wmm if k == 0 else [], "pqkv" if k == 31 else None)
                mm_c = p.c("pqkv")
                p.emit("scalar", lambda e, o=o, T=T, bank=bank: e.activation(
                    t["qkvT"][:, 2048 * o + 512 * T:2048 * o + 512 * T + 512],
                    t["pqkv"][bank][:], AF.Silu), [("pqkv", mm_c)], "sqkv")
        sq_all = p.c("sqkv")

        for hpair in range(2):
          for i in range(NB):
            for z in range(2):
                hc = 2 * hpair + z
                o_q, o_k, o_v = 3 * hc, 3 * hc + 1, 3 * hc + 2
                zz = str(z)
                cb = 256 * i
                wqd = [("sqkv", sq_all), ("ald", ald_all), ("am3" + zz, p.c("am3" + zz))]
                p.emit("vector", lambda e, o_q=o_q, cb=cb, hc=hc, z=z: e.tensor_tensor(
                    t["qd_sb"][z][:], t["qkvT"][:, 2048 * o_q + cb:2048 * o_q + cb + 256],
                    t["qdec_sb"][:, 256 * hc:256 * hc + 256], ALU.mult), wqd, "aev" + zz)
                wtr = [("sqkv", sq_all), ("ld", ident_c), ("atrev" + zz, p.c("atrev" + zz))]
                for s2 in range(2):
                    p.emit("tensor", lambda e, o_k=o_k, cb=cb, s2=s2, z=z: e.transpose(
                        t["ptr"][z][:, 128 * s2:128 * s2 + 128],
                        t["qkvT"][:, 2048 * o_k + cb + 128 * s2:2048 * o_k + cb + 128 * s2 + 128],
                        t["ident_sb"][:]), wtr if s2 == 0 else [], None)
                    p.emit("tensor", lambda e, o_v=o_v, cb=cb, s2=s2, z=z: e.transpose(
                        t["ptr"][z][:, 256 + 128 * s2:256 + 128 * s2 + 128],
                        t["qkvT"][:, 2048 * o_v + cb + 128 * s2:2048 * o_v + cb + 128 * s2 + 128],
                        t["ident_sb"][:]), [], ("atr" + zz) if s2 == 1 else None)
                atr_c = p.c("atr" + zz)
                for s2 in range(2):
                    p.emit("vector", lambda e, s2=s2, hc=hc, z=z: e.tensor_scalar(
                        t["kk_sb"][z][:, 128 * s2:128 * s2 + 128], t["ptr"][z][:, 128 * s2:128 * s2 + 128],
                        t["kdec_sb"][:, 2 * hc + s2:2 * hc + s2 + 1], None, ALU.mult),
                        [("atr" + zz, atr_c)], "atrev" + zz)
                    p.emit("vector", lambda e, s2=s2, z=z: e.tensor_copy(
                        t["vv_sb"][z][:, 128 * s2:128 * s2 + 128],
                        t["ptr"][z][:, 256 + 128 * s2:256 + 128 * s2 + 128]),
                        [("atr" + zz, atr_c)], "atrev" + zz)
                atrev_c = p.c("atrev" + zz)
                wm1 = [("am2" + zz, p.c("am2" + zz))]
                for ms in range(2):
                    p.emit("tensor", lambda e, o_k=o_k, o_q=o_q, cb=cb, ms=ms, z=z: e.matmul(
                        t["ps_s"][z][:, 256 * ms:256 * ms + 256],
                        t["qkvT"][:, 2048 * o_k + cb + 128 * ms:2048 * o_k + cb + 128 * ms + 128],
                        t["qkvT"][:, 2048 * o_q + cb:2048 * o_q + cb + 256],
                        start=True, stop=True), wm1 if ms == 0 else [], ("am1" + zz) if ms == 1 else None)
                am1_c = p.c("am1" + zz)
                for ms in range(2):
                    p.emit("vector", lambda e, ms=ms, hc=hc, z=z: e.tensor_tensor(
                        t["scs_sb"][z][:, 256 * ms:256 * ms + 256], t["ps_s"][z][:, 256 * ms:256 * ms + 256],
                        t["ddT_sb"][:, 512 * hc + 256 * ms:512 * hc + 256 * ms + 256], ALU.mult),
                        [("am1" + zz, am1_c)], ("am2" + zz) if ms == 1 else None)
                am2_c = p.c("am2" + zz)
                wm3 = [("am2" + zz, am2_c), ("aev" + zz, p.c("aev" + zz)), ("atrev" + zz, atrev_c),
                       ("aoev" + zz, p.c("aoev" + zz))]
                first = True
                for ns in range(2):
                    for ms in range(2):
                        last0 = (i == 0 and ms == 1)
                        p.emit("tensor", lambda e, ns=ns, ms=ms, last0=last0, z=z: e.matmul(
                            t["ps_att"][z][:, 128 * ns:128 * ns + 128],
                            t["scs_sb"][z][:, 256 * ms + 128 * ns:256 * ms + 128 * ns + 128],
                            t["vv_sb"][z][:, 128 * ms:128 * ms + 128],
                            start=(ms == 0), stop=last0), wm3 if first else [],
                            ("am3" + zz) if (last0 and ns == 1) else None)
                        first = False
                    if i > 0:
                        p.emit("tensor", lambda e, ns=ns, z=z: e.matmul(
                            t["ps_att"][z][:, 128 * ns:128 * ns + 128],
                            t["qd_sb"][z][:, 128 * ns:128 * ns + 128], t["kv_bf"][z][:],
                            start=False, stop=True), [("akvc" + zz, p.c("akvc" + zz))],
                            ("am3" + zz) if ns == 1 else None)
                am3_c = p.c("am3" + zz)
                wm5 = [("akv" + zz, p.c("akv" + zz))]
                for ms in range(2):
                    p.emit("tensor", lambda e, ms=ms, z=z: e.matmul(
                        t["ps_kv"][z][:, 0:128], t["kk_sb"][z][:, 128 * ms:128 * ms + 128],
                        t["vv_sb"][z][:, 128 * ms:128 * ms + 128],
                        start=(ms == 0), stop=(ms == 1)), wm5 if ms == 0 else [],
                        ("am5" + zz) if ms == 1 else None)
                am5_c = p.c("am5" + zz)
                for ns in range(2):
                    p.emit("vector", lambda e, ns=ns, z=z: e.tensor_copy(
                        t["att_sb"][z][:, 128 * ns:128 * ns + 128], t["ps_att"][z][:, 128 * ns:128 * ns + 128]),
                        [("am3" + zz, am3_c), ("ast", p.c("ast"))], ("aoev" + zz) if ns == 1 else None)
                aoev_c = p.c("aoev" + zz)
                if i == 0:
                    p.emit("vector", lambda e, z=z: e.tensor_copy(t["kv_sb"][z][:], t["ps_kv"][z][:, 0:128]),
                           [("am5" + zz, am5_c)], "akv" + zz)
                else:
                    p.emit("vector", lambda e, hc=hc, z=z: e.scalar_tensor_tensor(
                        t["kv_sb"][z][:], t["kv_sb"][z][:], t["kdec_sb"][:, 8 + hc:8 + hc + 1],
                        t["ps_kv"][z][:, 0:128], ALU.mult, ALU.add),
                        [("am5" + zz, am5_c)], "akv" + zz)
                akv_c = p.c("akv" + zz)
                if i < NB - 1:
                    p.emit("vector", lambda e, z=z: e.tensor_copy(t["kv_bf"][z][:], t["kv_sb"][z][:]),
                           [("akv" + zz, akv_c)], "akvc" + zz)
                colb = 128 * hc
                p.dma("sync",
                      t["ag2_in"][2048 * b + cb:2048 * b + cb + 256, colb:colb + 128]
                      .rearrange("(s r) c -> r s c", r=128),
                      t["att_sb"][z][:].rearrange("p (s c) -> p s c", c=128),
                      "ast", [("aoev" + zz, aoev_c)])
        p.emit("gpsimd", lambda e, b=b: e.collective_compute(
            "AllGather", ALU.bypass, replica_groups=[list(range(NCORES))],
            ins=[t["ag2_in"][2048 * b:2048 * b + 2048, :]],
            outs=[t["ag2_out"][16384 * b:16384 * b + 16384, :]]),
            [("ast", p.c("ast"))], "cc3")
        if b == 0:
            for q in range(4):
                p.dma("gpsimd", t["gate_bf"][1024 * q:1024 * q + 1024, :],
                      t["gate_wT"][1024 * q:1024 * q + 1024, :], "wcast2")
                p.dma("gpsimd", t["out_bf"][1024 * q:1024 * q + 1024, :],
                      t["out_wT"][1024 * q:1024 * q + 1024, :], "wcast2")


def emit_tail(p, t):
    nc = p.nc
    p.dma("sync", t["cid_sb"][:], t["core_c"], "ld")
    p.dma("sync", t["aoff_sb"][:], t["ag_off"], "ld")
    p.dma("sync", t["nw_sb"][:], t["norm_w_t"], "ld")
    ld_all = p.c("ld")

    regs = {}

    def emit_dyn_dma(engname, outap, make_in, wait, inc):
        def fn(e):
            if engname not in regs:
                regs[engname] = nc.values_load(t["cid_sb"][0:1, 0:1], engines=[e.engine],
                                               min_val=0, max_val=7,
                                               skip_runtime_bounds_check=True)
            return e.dma_start(out=outap, in_=make_in(regs[engname]))
        return p.emit(engname, fn, wait, inc, 16)

    emit_dyn_dma("sync", t["xno_dram"][:, :],
                 lambda reg: t["ag1_out"][bass.DynSlice(reg * 4096, 4096), :],
                 [("cc1", 1), ("ld", ld_all)], "txl")
    p.dma("sync", t["xnoT"][:].rearrange("p (k t) -> p k t", t=1024),
          t["xno_dram"][:, :].rearrange("(k p) t -> p k t", p=128),
          "txl", [("txl", p.c("txl"))])
    txl_c = p.c("txl")

    for jt in range(32):
        gb = jt % 2
        wgl = [("wcast2", p.c("wcast2")), ("gmm", max(0, p.c("gmm") - 1))]
        p.dma("sync", t["gw_t"][gb][:].rearrange("p (k o) -> p k o", o=128),
              t["gate_bf"][:, 128 * jt:128 * jt + 128].rearrange("(k p) o -> p k o", p=128),
              "gwl", wgl)
        gwl_c = p.c("gwl")
        wmm = [("gwl", gwl_c), ("txl", txl_c), ("gsg", p.c("gsg"))]
        for th in range(2):
            for k in range(32):
                p.emit("tensor", lambda e, k=k, th=th, gb=gb: e.matmul(
                    t["pgate"][th][:], t["gw_t"][gb][:, 128 * k:128 * k + 128],
                    t["xnoT"][:, 1024 * k + 512 * th:1024 * k + 512 * th + 512],
                    start=(k == 0), stop=(k == 31)),
                    wmm if (k == 0 and th == 0) else [],
                    "gmm" if (k == 31 and th == 1) else None)
        gmm_c = p.c("gmm")
        for th in range(2):
            p.emit("scalar", lambda e, jt=jt, th=th: e.activation(
                t["pnT"][:, 1024 * jt + 512 * th:1024 * jt + 512 * th + 512],
                t["pgate"][th][:], AF.Sigmoid),
                [("gmm", gmm_c)], "gsg")
    gsg_all = p.c("gsg")

    regs2 = {}

    def emit_dyn2(engname, outap, make_in, wait, inc):
        def fn(e):
            if engname not in regs2:
                regs2[engname] = nc.values_load(t["aoff_sb"][0:1, 0:1], engines=[e.engine],
                                                min_val=0, max_val=50176,
                                                skip_runtime_bounds_check=True)
            return e.dma_start(out=outap, in_=make_in(regs2[engname]))
        return p.emit(engname, fn, wait, inc, 16)

    wex0 = [("cc3", 4), ("ld", ld_all)]
    for r in range(NCORES):
        emit_dyn2("sync", t["att_dram"][:, 512 * r:512 * r + 512],
                  lambda reg, r=r: t["ag2_out"][
                      bass.DynSlice(reg + 2048 * r, 1024), :],
                  wex0 if r == 0 else [], "tex0")
    tex0_c = p.c("tex0")

    for ti in range(8):
        ab = ti % 2
        wex = [("tex0", tex0_c), ("trm", p.c("trm")), ("tsc", p.c("tsc"))]
        p.dma("sync", t["att_t"][ab][:], t["att_dram"][128 * ti:128 * ti + 128, :], "tex", wex)
        tex_c = p.c("tex")
        p.emit("scalar", lambda e, ab=ab: e.activation(
            t["tmp_t"][:], t["att_t"][ab][:], AF.Square, accum_out=t["ssq"][:]),
            [("tex", tex_c), ("ttr", p.c("ttr"))], "tsc")
        p.emit("scalar", lambda e: e.activation(t["std_c"][:], t["ssq"][:], AF.Sqrt,
                                                bias=np.float32(1e-5), scale=np.float32(1.0 / HID)),
               [("tsc", p.c("tsc"))], "tsc")
        st_c = p.c("tsc")
        p.emit("vector", lambda e: e.reciprocal(t["rstd_c"][:], t["std_c"][:]), [("tsc", st_c)], "trm")
        r_c = p.c("trm")
        p.emit("vector", lambda e, ab=ab: e.tensor_scalar(
            t["tmp_t"][:], t["att_t"][ab][:], t["rstd_c"][:], None, ALU.mult),
            [("trm", r_c), ("ttr", p.c("ttr"))], "trm")
        tmp_c = p.c("trm")
        for grp in range(4):
            bank = grp % 2
            wtr = [("trm", tmp_c), ("tte", max(0, p.c("tte") - 1))]
            for j in range(8):
                jc = grp * 8 + j
                p.emit("tensor", lambda e, jc=jc, j=j, bank=bank: e.transpose(
                    t["ptr2"][bank][:, 128 * j:128 * j + 128],
                    t["tmp_t"][:, 128 * jc:128 * jc + 128],
                    t["ident_sb"][:]), wtr if j == 0 else [], "ttr" if j == 7 else None)
            tr_c = p.c("ttr")
            for j in range(8):
                jc = grp * 8 + j
                p.emit("vector", lambda e, jc=jc, j=j, ti=ti, bank=bank: e.scalar_tensor_tensor(
                    t["pnT"][:, 1024 * jc + 128 * ti:1024 * jc + 128 * ti + 128],
                    t["ptr2"][bank][:, 128 * j:128 * j + 128],
                    t["nw_sb"][:, jc:jc + 1],
                    t["pnT"][:, 1024 * jc + 128 * ti:1024 * jc + 128 * ti + 128],
                    ALU.mult, ALU.mult),
                    [("ttr", tr_c), ("gsg", gsg_all)], "tte" if j == 7 else None)
    norm_done = p.c("tte")

    for ho in range(32):
        ob = ho % 2
        wol = [("omm", max(0, p.c("omm") - 1)), ("gmm", p.c("gmm") if ho < 2 else 0)]
        p.dma("sync", t["ow_t"][ob][:].rearrange("p (k o) -> p k o", o=128),
              t["out_bf"][:, 128 * ho:128 * ho + 128].rearrange("(k p) o -> p k o", p=128),
              "owl", wol)
        owl_c = p.c("owl")
        wmm = [("owl", owl_c), ("tte", p.c("tte")), ("osv", p.c("osv"))]
        for th in range(2):
            for k in range(32):
                p.emit("tensor", lambda e, k=k, th=th, ob=ob: e.matmul(
                    t["pout"][th][:], t["ow_t"][ob][:, 128 * k:128 * k + 128],
                    t["pnT"][:, 1024 * k + 512 * th:1024 * k + 512 * th + 512],
                    start=(k == 0), stop=(k == 31)),
                    wmm if (k == 0 and th == 0) else [],
                    "omm" if (k == 31 and th == 1) else None)
        omm_c = p.c("omm")
        for th in range(2):
            p.emit("vector", lambda e, th=th: e.tensor_copy(
                t["sT_scr"][:, 512 * th:512 * th + 512], t["pout"][th][:]),
                [("omm", omm_c), ("ttr2", p.c("ttr2"))], "osv")
        osv_c = p.c("osv")
        bank2 = ho % 2
        wtr = [("osv", osv_c), ("tte2", max(0, p.c("tte2") - 1))]
        for j in range(8):
            p.emit("tensor", lambda e, j=j, bank2=bank2: e.transpose(
                t["ptr2"][bank2][:, 128 * j:128 * j + 128],
                t["sT_scr"][:, 128 * j:128 * j + 128],
                t["ident_sb"][:]), wtr if j == 0 else [], "ttr2" if j == 7 else None)
        tr_c = p.c("ttr2")
        p.emit("vector", lambda e, ho=ho, bank2=bank2: e.tensor_copy(
            t["s_buf"][:].rearrange("p (tt c) -> p tt c", c=4096)[:, :, 128 * ho:128 * ho + 128],
            t["ptr2"][bank2][:].rearrange("p (j c) -> p j c", c=128)),
            [("ttr2", tr_c)], "tte2")
    s_done = p.c("tte2")

    for t2 in range(16):
        ti, hh = t2 // 2, t2 % 2
        fb = t2 % 2
        wfl = [("tte2", s_done), ("cc2", 4), ("fin", max(0, p.c("fin") - 2)), ("tsc2", p.c("tsc2"))]
        p.dma("sync", t["d_t"][fb][:],
              t["rs_out"][128 * ti:128 * ti + 128, 2048 * hh:2048 * hh + 2048], "tdl", wfl)
        p.dma("sync", t["xf_t"][fb][:],
              t["x_own"][128 * ti:128 * ti + 128, 2048 * hh:2048 * hh + 2048], "tdl", [])
        tdl_c = p.c("tdl")
        p.emit("scalar", lambda e, fb=fb: e.activation(
            t["g_t"][:], t["d_t"][fb][:], AF.Silu),
            [("tdl", tdl_c), ("fin", max(0, p.c("fin") - 1))], "tsc2")
        g_c = p.c("tsc2")
        p.emit("vector", lambda e, ti=ti, hh=hh: e.tensor_tensor(
            t["sg_t"][:], t["s_buf"][:, 4096 * ti + 2048 * hh:4096 * ti + 2048 * hh + 2048],
            t["g_t"][:], ALU.mult), [("tsc2", g_c), ("fst", p.c("fst"))], "fin")
        p.emit("vector", lambda e, fb=fb: e.tensor_add(
            t["sg_t"][:], t["sg_t"][:], t["xf_t"][fb][:]), [("fin", p.c("fin"))], "fin")
        fin_c = p.c("fin")
        p.dma("sync", t["out"][128 * ti:128 * ti + 128, 2048 * hh:2048 * hh + 2048],
              t["sg_t"][:], "fst", [("fin", fin_c)])
    p.emit("sync", lambda e: e.nop(), [("fst", p.c("fst"))], None)


SEMS = ("ld", "xld", "p0s", "p0v", "p0t", "p0e", "agi", "cc1",
        "prep", "qb0", "qb1", "nib", "sc", "deq", "x", "sg", "sgc", "gu", "sil", "hT", "hts0", "hts1",
        "qbd", "nibd", "scd_", "deqd", "htl0", "htl1", "sh", "shc", "pd", "rsv", "rss0", "rss1", "cc2",
        "ald", "qwl", "ax", "pqkv", "sqkv", "aev0", "aev1", "atr0", "atr1", "atrev0", "atrev1",
        "am10", "am11", "am20", "am21", "am30", "am31", "am50", "am51",
        "aoev0", "aoev1", "akv0", "akv1", "akvc0", "akvc1", "ast", "cc3",
        "txl", "tex", "tsc", "trm", "ttr", "tte", "gwl", "gmm", "gsg", "tpt",
        "owl", "omm", "osv", "ttr2", "tte2", "tdl", "tsc2", "fin", "fst", "tex0", "wcast", "wcast2")

NO_GATE = ("cc1", "cc2", "cc3")


def build_nc():
    nc = bass.Bass("TRN2", target_bir_lowering=False)
    t = {}
    EI = lambda n, sh, dt: t.__setitem__(n, nc.dram_tensor(n, sh, dt, kind="ExternalInput").ap())
    EI("x_own", [TOK_PER, HID], F32)
    EI("qkv_wT", [HID, 1536], F32)
    EI("gate_wT", [HID, HID], F32)
    EI("out_wT", [HID, HID], F32)
    EI("norm_w_t", [128, 32], F32)
    EI("qb_gu", [2048, 2752], U8)
    EI("sc_gu", [32, 2752], F32)
    EI("qz_gu", [32, 344], U32)
    EI("ind_hid_f32", [32, HID], F32)
    EI("ind_hid_b", [128, 1024], BF16)
    EI("qb_down", [BETA_PER, 4096], U8)
    EI("sc_down", [12, 4096], F32)
    EI("qz_down", [12, 512], U32)
    EI("ind_dn_f32", [12, F_PER], F32)
    EI("ind_dn_b", [128, 144], BF16)
    EI("ident", [128, 128], BF16)
    EI("qdec", [128, 1024], BF16)
    EI("kdec", [128, 12], F32)
    EI("ddT", [128, 2048], BF16)
    EI("core_c", [1, 1], I32)
    EI("ag_off", [1, 1], I32)
    t["out"] = nc.dram_tensor("out", [TOK_PER, HID], F32, kind="ExternalOutput").ap()

    t["ag1_in_h"] = nc.dram_tensor("ag1_in", [HID, TOK_PER], BF16, kind="Internal")
    t["ag1_out_h"] = nc.dram_tensor("ag1_out", [NCORES * HID, TOK_PER], BF16, kind="Internal")
    t["ag2_in_h"] = nc.dram_tensor("ag2_in", [TOK, 512], BF16, kind="Internal")
    t["ag2_out_h"] = nc.dram_tensor("ag2_out", [NCORES * TOK, 512], BF16, kind="Internal")
    t["rs_in_h"] = nc.dram_tensor("rs_in", [TOK, HID], BF16, kind="Internal")
    t["rs_out_h"] = nc.dram_tensor("rs_out", [TOK_PER, HID], BF16, kind="Internal")
    t["hT_dram_h"] = nc.dram_tensor("hT_dram", [1536, TOK], BF16, kind="Internal")
    t["xno_dram_h"] = nc.dram_tensor("xno_dram", [HID, TOK_PER], BF16, kind="Internal")
    t["qkv_bf_h"] = nc.dram_tensor("qkv_bf", [HID, 1536], BF16, kind="Internal")
    t["gate_bf_h"] = nc.dram_tensor("gate_bf", [HID, HID], BF16, kind="Internal")
    t["out_bf_h"] = nc.dram_tensor("out_bf", [HID, HID], BF16, kind="Internal")
    t["att_dram_h"] = nc.dram_tensor("att_dram", [TOK_PER, HID], BF16, kind="Internal")
    for n in ("ag1_in", "ag1_out", "ag2_in", "ag2_out", "rs_in", "rs_out", "hT_dram",
              "xno_dram", "att_dram", "qkv_bf", "gate_bf", "out_bf"):
        t[n] = t[n + "_h"].ap()

    stack = contextlib.ExitStack()
    S = lambda st, name, shape, dt: st.enter_context(nc.sbuf_tensor(name, shape, dt))
    P = lambda st, name, shape, dt=F32: st.enter_context(nc.psum_tensor(name, shape, dt))

    t["ident_sb"] = S(stack, "ident_sb", [128, 128], BF16)
    t["cid_sb"] = S(stack, "cid_sb", [1, 1], I32)
    t["aoff_sb"] = S(stack, "aoff_sb", [1, 1], I32)
    t["nw_sb"] = S(stack, "nw_sb", [128, 32], F32)
    for n in ("ssq", "mean_c", "msq_c", "var_c", "std_c", "rstd_c", "nmr_c"):
        t[n] = S(stack, n, [128, 1], F32)

    p = Prog(nc)
    with nc.Block() as block:
        for s in SEMS:
            p.sem(stack, s)

        def gate():
            p.set_gate([(s, p.c(s)) for s in SEMS if s not in NO_GATE and p.c(s) > 0])

        with contextlib.ExitStack() as st0:
            t["x_sb"] = [S(st0, "x_sb0", [128, HID], F32), S(st0, "x_sb1", [128, HID], F32)]
            t["sq_scr"] = S(st0, "sq_scr", [128, HID], F32)
            t["xn_sb"] = S(st0, "xn_sb", [128, HID], BF16)
            t["stg_sb"] = S(st0, "stg_sb", [128, HID], BF16)
            t["ptr0"] = [P(st0, "ptr00", [128, 1024], BF16), P(st0, "ptr01", [128, 1024], BF16)]
            emit_phase0(p, t)

        with contextlib.ExitStack() as stf:
            t["w_sb"] = S(stf, "w_sb", [128, 49152], BF16)
            _qq = S(stf, "qq", [128, 4096], U8)
            _nibA = S(stf, "nibA", [128, 4096], U8)
            _nibB = S(stf, "nibB", [128, 4096], U8)
            _qq2 = S(stf, "qq2", [128, 1536], U8)
            _nibA2 = S(stf, "nibA2", [128, 1536], U8)
            _nibB2 = S(stf, "nibB2", [128, 1536], U8)
            t["qbs"] = [_qq, _qq2]
            t["nibs0"] = [_nibA, _nibA2]
            t["nibs1"] = [_nibB, _nibB2]
            t["qbd_sb"], t["nibd0"], t["nibd1"] = _qq, _nibA, _nibB
            t["wd_sb"] = t["w_sb"]
            t["hT_sb"] = [S(stf, "hT_sb0", [128, 512], BF16), S(stf, "hT_sb1", [128, 512], BF16)]
            t["psc"] = [P(stf, "psc0", [128, 512]), P(stf, "psc1", [128, 512])]
            t["psg"] = P(stf, "psg", [32, 512])
            t["pg"] = [P(stf, "pg0", [128, 512]), P(stf, "pg1", [128, 512])]
            t["pu"] = [P(stf, "pu0", [128, 512]), P(stf, "pu1", [128, 512])]
            t["psh"] = P(stf, "psh", [12, 512])
            t["pd_"] = [t["pg"][0], t["pg"][1], t["pu"][0], t["pu"][1]]
            gate()
            with contextlib.ExitStack() as stg:
                t["sc_sb"] = S(stg, "sc_sb", [32, 2752], F32)
                t["qzt"] = S(stg, "qzt", [32, 344], U32)
                t["zq8"] = S(stg, "zq8", [32, 2752], U8)
                t["nscz_g"] = S(stg, "nscz_g", [32, F_PER], BF16)
                t["nscz_u"] = S(stg, "nscz_u", [32, F_PER], BF16)
                t["indf"] = S(stg, "indf", [32, HID], F32)
                t["indb"] = S(stg, "indb", [128, 1024], BF16)
                t["xt"] = S(stg, "xt", [128, 32 * 512], BF16)
                t["sg_sb"] = S(stg, "sg_sb", [32, 512], BF16)
                t["silu_t"] = [S(stg, "silu_t0", [128, 512], BF16), S(stg, "silu_t1", [128, 512], BF16)]
                gu_all, hts_all = emit_ffn_gu(p, t)
            with contextlib.ExitStack() as std_:
                t["scd"] = S(std_, "scd", [12, 4096], F32)
                t["qzd"] = S(std_, "qzd", [12, 512], U32)
                t["zqd8"] = S(std_, "zqd8", [12, 4096], U8)
                t["nsczd"] = S(std_, "nsczd", [12, 4096], BF16)
                t["inddf"] = S(std_, "inddf", [12, F_PER], F32)
                t["inddb"] = S(std_, "inddb", [128, 144], BF16)
                t["ht"] = [S(std_, "ht0", [128, 12 * 128], BF16),
                           S(std_, "ht1", [128, 12 * 128], BF16)]
                t["sh_sb"] = S(std_, "sh_sb", [12, 128], BF16)
                t["rs_sb"] = [S(std_, "rs_sb0", [128, HID], BF16), S(std_, "rs_sb1", [128, HID], BF16)]
                emit_ffn_down(p, t, gu_all, hts_all)

        with contextlib.ExitStack() as sta:
            t["qdec_sb"] = S(sta, "qdec_sb", [128, 1024], BF16)
            t["kdec_sb"] = S(sta, "kdec_sb", [128, 12], F32)
            t["ddT_sb"] = S(sta, "ddT_sb", [128, 2048], BF16)
            t["qkvw"] = S(sta, "qkvw", [128, 32 * 1536], BF16)
            _ax = S(sta, "axt0", [128, 32 * 512], BF16)
            t["axt"] = [_ax, _ax]
            t["qkvT"] = S(sta, "qkvT", [128, 12 * 2048], BF16)
            t["qd_sb"] = [S(sta, "qd_sb0", [128, 256], BF16), S(sta, "qd_sb1", [128, 256], BF16)]
            t["kk_sb"] = [S(sta, "kk_sb0", [128, 256], BF16), S(sta, "kk_sb1", [128, 256], BF16)]
            t["vv_sb"] = [S(sta, "vv_sb0", [128, 256], BF16), S(sta, "vv_sb1", [128, 256], BF16)]
            t["scs_sb"] = [S(sta, "scs_sb0", [128, 512], BF16), S(sta, "scs_sb1", [128, 512], BF16)]
            t["att_sb"] = [S(sta, "att_sb0", [128, 256], BF16), S(sta, "att_sb1", [128, 256], BF16)]
            t["kv_sb"] = [S(sta, "kv_sb0", [128, 128], F32), S(sta, "kv_sb1", [128, 128], F32)]
            t["kv_bf"] = [S(sta, "kv_bf0", [128, 128], BF16), S(sta, "kv_bf1", [128, 128], BF16)]
            _ps0 = P(sta, "pqkv0", [128, 512])
            _ps1 = P(sta, "pqkv1", [128, 512])
            t["pqkv"] = [_ps0, _ps1]
            t["ps_s"] = [_ps0, _ps1]
            t["ptr"] = [P(sta, "ptr0", [128, 1024], BF16), P(sta, "ptr1", [128, 1024], BF16)]
            t["ps_att"] = [P(sta, "ps_att0", [128, 512]), P(sta, "ps_att1", [128, 512])]
            t["ps_kv"] = [P(sta, "ps_kv0", [128, 512]), P(sta, "ps_kv1", [128, 512])]
            gate()
            emit_attn(p, t)

        with contextlib.ExitStack() as stt:
            t["xnoT"] = S(stt, "xnoT", [128, 32 * 1024], BF16)
            t["s_buf"] = t["xnoT"]
            t["pnT"] = S(stt, "pnT", [128, 32 * 1024], BF16)
            _at = S(stt, "att_t0", [128, HID], BF16)
            t["att_t"] = [_at, _at]
            t["tmp_t"] = S(stt, "tmp_t", [128, HID], BF16)
            t["gsig"] = S(stt, "gsig", [128, 1024], BF16)
            _gw0 = S(stt, "gw_t0", [128, 32 * 128], BF16)
            _gw1 = S(stt, "gw_t1", [128, 32 * 128], BF16)
            t["gw_t"] = [_gw0, _gw1]
            t["ow_t"] = [_gw0, _gw1]
            t["sT_scr"] = S(stt, "sT_scr", [128, 1024], BF16)
            _dt = S(stt, "d_t0", [128, 2048], BF16)
            t["d_t"] = [_dt, _dt]
            t["xf_t"] = [S(stt, "x_sb2", [128, 2048], F32), S(stt, "x_sb3", [128, 2048], F32)]
            t["g_t"] = S(stt, "g_t", [128, 2048], BF16)
            t["sg_t"] = S(stt, "sg_t", [128, 2048], F32)
            t["pgate"] = [P(stt, "pgate0", [128, 512]), P(stt, "pgate1", [128, 512])]
            t["pout"] = [P(stt, "pout0", [128, 512]), P(stt, "pout1", [128, 512])]
            t["ptr2"] = [P(stt, "ptr20", [128, 1024], BF16), P(stt, "ptr21", [128, 1024], BF16)]
            gate()
            emit_tail(p, t)

        p.replay(block)
    stack.close()
    return nc


def _host_prep(core, inputs):
    c = core
    qw_gu = np.asarray(inputs["L0_ffn_qweight_gu"])
    qz_gu = np.asarray(inputs["L0_ffn_qzeros_gu"])
    sc_gu = np.asarray(inputs["L0_ffn_scales_gu"])
    gidx_gu = np.asarray(inputs["L0_ffn_g_idx_gu"])
    qw_d = np.asarray(inputs["L0_ffn_qweight_down"])
    qz_d = np.asarray(inputs["L0_ffn_qzeros_down"])
    sc_d = np.asarray(inputs["L0_ffn_scales_down"])
    gidx_d = np.asarray(inputs["L0_ffn_g_idx_down"])

    qb_gu_full = _host_prep.qb_gu_full
    qb_d_full = _host_prep.qb_d_full
    cols_nat = np.concatenate([np.arange(F_PER * c, F_PER * (c + 1)),
                               INTER + np.arange(F_PER * c, F_PER * (c + 1))])
    perm_f = np.array([f_local(t5, par, pp) for (t5, par, K, off) in TILES for pp in range(K)])
    cols_sig = np.concatenate([cols_nat[:F_PER][perm_f], cols_nat[F_PER:][perm_f]])
    qb_gu_c = np.ascontiguousarray(qb_gu_full[:, cols_sig])
    sc_gu_c = np.ascontiguousarray(sc_gu[:, cols_sig])
    w0 = (F_PER * c) // 8
    qz_gu_c = np.ascontiguousarray(
        np.concatenate([qz_gu[:, w0:w0 + F_PER // 8],
                        qz_gu[:, INTER // 8 + w0:INTER // 8 + w0 + F_PER // 8]], axis=1))
    qb_d_c = np.ascontiguousarray(qb_d_full[BETA_PER * c:BETA_PER * (c + 1), :])
    g0 = (F_PER * c) // GROUP
    ng = min(g0 + 12, N_DG) - g0
    sc_d_c = np.zeros((12, 4096), np.float32)
    sc_d_c[:ng] = sc_d[g0:g0 + ng]
    qz_d_c = np.zeros((12, 512), np.uint32)
    qz_d_c[:ng] = qz_d[g0:g0 + ng]

    ind_dn_f32 = np.zeros((12, F_PER), np.float32)
    ind_dn_b = np.zeros((128, 144), ml_dtypes.bfloat16)
    for kt, (t5, par, K, off) in enumerate(TILES):
        for pp in range(K):
            g = gidx_d[F_PER * c + f_local(t5, par, pp)] - g0
            if 0 <= g < 12:
                ind_dn_f32[g, off + pp] = 1.0
                ind_dn_b[pp, 12 * kt + g] = 1.0

    qdec = np.zeros((128, 1024), ml_dtypes.bfloat16)
    kdec = np.zeros((128, 12), np.float32)
    ddT = np.zeros((128, 2048), ml_dtypes.bfloat16)
    for hc in range(4):
        qd, kd, dd, bd = decay_tables(4 * c + hc)
        qdec[:, 256 * hc:256 * (hc + 1)] = np.broadcast_to(qd[None, :], (128, 256)).astype(ml_dtypes.bfloat16)
        kdec[:, 2 * hc] = kd[:128]
        kdec[:, 2 * hc + 1] = kd[128:]
        kdec[:, 8 + hc] = bd
        for ms in range(2):
            ddT[:, 512 * hc + 256 * ms:512 * hc + 256 * (ms + 1)] = \
                dd[128 * ms:128 * (ms + 1), :].astype(ml_dtypes.bfloat16)

    qkv_cols = np.arange(1536 * c, 1536 * (c + 1))
    qkv_wT_c = np.ascontiguousarray(_host_prep.qkv_wT_perm[:, qkv_cols])

    return dict(
        x_own=_host_prep.x_flat[TOK_PER * c:TOK_PER * (c + 1)],
        qkv_wT=qkv_wT_c,
        gate_wT=_host_prep.gate_wT_perm,
        out_wT=_host_prep.out_wT,
        norm_w_t=_host_prep.norm_w_t,
        qb_gu=qb_gu_c, sc_gu=sc_gu_c, qz_gu=qz_gu_c,
        ind_hid_f32=_host_prep.ind_hid_f32, ind_hid_b=_host_prep.ind_hid_b,
        qb_down=qb_d_c, sc_down=sc_d_c, qz_down=qz_d_c,
        ind_dn_f32=ind_dn_f32, ind_dn_b=ind_dn_b,
        ident=_host_prep.ident, qdec=qdec, kdec=kdec, ddT=ddT,
        core_c=np.array([[c]], np.int32),
        ag_off=np.array([[16384 * (c // 2) + 1024 * (c % 2)]], np.int32),
    )


def kernel(**inputs):
    x = np.asarray(inputs["L0_attn_hidden_states"])
    _host_prep.x_flat = np.ascontiguousarray(x.reshape(TOK, HID))
    qw_gu = np.asarray(inputs["L0_ffn_qweight_gu"])
    _host_prep.qb_gu_full = np.ascontiguousarray(
        qw_gu.view(np.uint8).reshape(512, 2 * INTER, 4).transpose(0, 2, 1).reshape(2048, 2 * INTER))
    qw_d = np.asarray(inputs["L0_ffn_qweight_down"])
    _host_prep.qb_d_full = np.ascontiguousarray(
        qw_d.view(np.uint8).reshape(F_PER, HID, 4).transpose(0, 2, 1).reshape(2 * INTER // 4, HID))
    _host_prep.qkv_wT_perm = np.ascontiguousarray(
        np.asarray(inputs["L0_attn_qkv_w"]).T[PERM_HID, :])
    _host_prep.gate_wT_perm = np.ascontiguousarray(
        np.asarray(inputs["L0_attn_gate_w"]).T[PERM_HID, :])
    _host_prep.out_wT = np.ascontiguousarray(np.asarray(inputs["L0_attn_out_w"]).T)
    _host_prep.norm_w_t = np.ascontiguousarray(
        np.asarray(inputs["L0_attn_norm_w"]).reshape(32, 128).T.astype(np.float32))
    gidx_gu = np.asarray(inputs["L0_ffn_g_idx_gu"])
    ind_hid_f32 = np.zeros((32, HID), np.float32)
    ind_hid_b = np.zeros((128, 1024), ml_dtypes.bfloat16)
    for kk in range(32):
        for pp in range(128):
            i_nat = PERM_HID[128 * kk + pp]
            g = gidx_gu[i_nat]
            ind_hid_f32[g, 128 * kk + pp] = 1.0
            ind_hid_b[pp, 32 * kk + g] = 1.0
    _host_prep.ind_hid_f32 = ind_hid_f32
    _host_prep.ind_hid_b = ind_hid_b
    _host_prep.ident = np.eye(128, dtype=ml_dtypes.bfloat16)

    nc = build_nc()
    in_maps = [_host_prep(c, inputs) for c in range(NCORES)]
    res = run_bass_kernel_spmd(nc, in_maps, core_ids=list(range(NCORES)))
    out = np.concatenate([res.results[c]["out"] for c in range(NCORES)], axis=0)
    return np.ascontiguousarray(out.reshape(NBATCH, SEQ, HID)).astype(np.float32)


# version 21 (final)
# speedup vs baseline: 1.0352x; 1.0352x over previous
import contextlib
import numpy as np
import ml_dtypes
import concourse.bass as bass
import concourse.mybir as mybir
from concourse.bass_utils import run_bass_kernel_spmd

F32 = mybir.dt.float32
BF16 = mybir.dt.bfloat16
U8 = mybir.dt.uint8
U32 = mybir.dt.uint32
I32 = mybir.dt.int32
AF = mybir.ActivationFunctionType
ALU = mybir.AluOpType

HID = 4096
INTER = 11008
GROUP = 128
NCORES = 8
F_PER = INTER // NCORES
BETA_PER = F_PER // 2
N_DG = 86
SEQ = 2048
NB = 8
BS = 256
NBATCH = 4
TOK = 8192
TOK_PER = TOK // NCORES

TILES = []
_off = 0
for _t5 in range(6):
    _K = 128 if _t5 < 5 else 48
    for _par in (0, 1):
        TILES.append((_t5, _par, _K, _off))
        _off += _K
assert _off == F_PER
BLOCKS = [TILES[0:6], TILES[6:12]]

PERM_HID = np.arange(HID).reshape(16, 128, 2).transpose(0, 2, 1).reshape(-1)


def f_local(t5, par, pp):
    return 256 * t5 + 2 * pp + par


def chunks(total, step):
    out = []
    o = 0
    while o < total:
        out.append((o, min(step, total - o)))
        o += step
    return out


def decay_tables(head_global):
    base = 1.0 / 2.0 ** (8.0 / 32)
    factor = 1.0 - 1.0 / (31 + 1e-5) + 1e-5
    slope = (base ** (head_global + 1)) * factor
    n = np.arange(BS)
    qdec = np.exp(-slope * (n + 1))
    kdec = np.exp(-slope * (BS - (n + 1)))
    dd = np.where(n[None, :] >= n[:, None], np.exp(-slope * (n[None, :] - n[:, None])), 0.0)
    bdec = float(np.exp(-slope * BS))
    return qdec, kdec, dd, bdec


class Prog:
    ENGINES = ("sync", "tensor", "vector", "scalar", "gpsimd")

    def __init__(self, nc):
        self.nc = nc
        self.events = {e: [] for e in self.ENGINES}
        self.counts = {}
        self.handles = {}
        self.gate = []

    def sem(self, stack, name):
        self.handles[name] = stack.enter_context(self.nc.semaphore(name))
        self.counts[name] = 0

    def set_gate(self, waits):
        self.gate = list(waits)

    def emit(self, eng, fn, wait=(), inc=None, inc_n=1):
        self.events[eng].append((self.gate + list(wait), fn, inc, inc_n))
        if inc is not None:
            self.counts[inc] += inc_n
        return self.counts.get(inc)

    def dma(self, eng, out, in_, inc, wait=()):
        return self.emit(eng, lambda e: e.dma_start(out=out, in_=in_), wait, inc, 16)

    def c(self, sem):
        return self.counts[sem]

    def snap(self, names):
        return [(s, self.counts[s]) for s in names]

    def replay(self, block):
        for eng_name in self.ENGINES:
            evs = self.events[eng_name]
            if not evs:
                continue

            def make(evs):
                def body(eng):
                    lastw = {}
                    for waits, fn, inc, inc_n in evs:
                        for s, cval in waits:
                            if cval <= 0 or lastw.get(s, -1) >= cval:
                                continue
                            eng.wait_ge(self.handles[s], cval)
                            lastw[s] = cval
                        inst = fn(eng)
                        if inc is not None:
                            inst.then_inc(self.handles[inc], inc_n)
                return body

            getattr(block, eng_name)(make(evs))


def emit_phase0(p, t):
    nc = p.nc
    p.dma("sync", t["ident_sb"][:], t["ident"], "ld")
    ident_c = p.c("ld")
    p.emit("vector", lambda e: e.memset(t["stg_sb"][:], 0.0), [], "p0v")
    z_c = p.c("p0v")
    for blk, half in ((10, 0), (10, 1), (11, 0), (11, 1)):
        p.dma("gpsimd", t["hT_dram"][128 * blk + 48:128 * blk + 128, 4096 * half:4096 * half + 4096],
              t["stg_sb"][0:80, :], "agi", [("p0v", z_c)])

    for ti in range(8):
        p.dma("sync", t["x_sb"][ti % 2][:], t["x_own"][128 * ti:128 * ti + 128, :], "xld",
              [("p0s", max(0, p.c("p0s") - 1))])
        xld_c = p.c("xld")
        p.emit("scalar", lambda e, ti=ti: e.activation(
            t["sq_scr"][:], t["x_sb"][ti % 2][:], AF.Square, accum_out=t["ssq"][:]),
            [("xld", xld_c), ("p0v", max(0, p.c("p0v") - 4))], "p0s")
        sq_c = p.c("p0s")
        p.emit("vector", lambda e, ti=ti: e.reduce_sum(
            t["mean_c"][:], t["x_sb"][ti % 2][:], mybir.AxisListType.X),
            [("xld", xld_c)], "p0v")
        p.emit("vector", lambda e: e.tensor_scalar_mul(t["mean_c"][:], t["mean_c"][:], 1.0 / HID),
               [("p0v", p.c("p0v"))], "p0v")
        p.emit("vector", lambda e: e.tensor_tensor(t["msq_c"][:], t["mean_c"][:], t["mean_c"][:], ALU.mult),
               [("p0v", p.c("p0v"))], "p0v")
        p.emit("vector", lambda e: e.scalar_tensor_tensor(
            t["var_c"][:], t["ssq"][:], 1.0 / HID, t["msq_c"][:], ALU.mult, ALU.subtract),
            [("p0s", sq_c), ("p0v", p.c("p0v"))], "p0v")
        v_c = p.c("p0v")
        p.emit("scalar", lambda e: e.activation(t["std_c"][:], t["var_c"][:], AF.Sqrt, bias=np.float32(1e-5)),
               [("p0v", v_c), ("p0s", p.c("p0s"))], "p0s")
        s_c = p.c("p0s")
        p.emit("vector", lambda e: e.reciprocal(t["rstd_c"][:], t["std_c"][:]),
               [("p0s", s_c), ("p0v", p.c("p0v"))], "p0v")
        p.emit("vector", lambda e: e.scalar_tensor_tensor(
            t["nmr_c"][:], t["mean_c"][:], -1.0, t["rstd_c"][:], ALU.mult, ALU.mult),
            [("p0v", p.c("p0v"))], "p0v")
        r_c = p.c("p0v")
        p.emit("scalar", lambda e, ti=ti: e.activation(
            t["xn_sb"][:], t["x_sb"][ti % 2][:], AF.Identity, bias=t["nmr_c"][:], scale=t["rstd_c"][:]),
            [("p0v", r_c), ("p0t", p.c("p0t"))], "p0s")
        xn_c = p.c("p0s")
        for grp in range(4):
            bank = grp % 2
            wtr = [("p0s", xn_c), ("ld", ident_c), ("p0e", max(0, p.c("p0e") - 1))]
            for j in range(8):
                kk = grp * 8 + j
                dg, par = kk // 2, kk % 2
                p.emit("tensor", lambda e, dg=dg, par=par, j=j, bank=bank: e.transpose(
                    t["ptr0"][bank][:, 128 * j:128 * j + 128],
                    t["xn_sb"][:, 256 * dg + par:256 * dg + 256:2],
                    t["ident_sb"][:]), wtr if j == 0 else [], "p0t" if j == 7 else None)
            tr_c = p.c("p0t")
            p.emit("vector", lambda e, grp=grp, bank=bank: e.tensor_copy(
                t["stg_sb"][:, 1024 * grp:1024 * grp + 1024], t["ptr0"][bank][:]),
                [("p0t", tr_c), ("agi", p.c("agi"))], "p0e")
        e_c = p.c("p0e")
        p.dma("gpsimd",
              t["ag1_in"][:, 128 * ti:128 * ti + 128].rearrange("(k p) t -> p k t", p=128),
              t["stg_sb"][:].rearrange("p (k t) -> p k t", t=128), "agi", [("p0e", e_c)])
    p.emit("gpsimd", lambda e: e.collective_compute(
        "AllGather", ALU.bypass, replica_groups=[list(range(NCORES))],
        ins=[t["ag1_in_h"].ap().opt()], outs=[t["ag1_out_h"].ap().opt()]),
        [("agi", p.c("agi"))], "cc1")



def emit_ffn_gu(p, t):
    nc = p.nc
    p.dma("sync", t["sc_sb"][:], t["sc_gu"], "ld")
    p.dma("sync", t["qzt"][:], t["qz_gu"], "ld")
    p.dma("sync", t["indf"][:], t["ind_hid_f32"], "ld")
    p.dma("sync", t["indb"][:], t["ind_hid_b"], "ld")
    ld_all = p.c("ld")
    wld = [("ld", ld_all)]
    p.emit("vector", lambda e: e.tensor_single_scalar(
        t["zq8"][:, 0:1376].bitcast(U32), t["qzt"][:].bitcast(U32), 0x0F0F0F0F, ALU.bitwise_and), wld, "prep")
    p.emit("vector", lambda e: e.tensor_scalar(
        t["zq8"][:, 1376:2752].bitcast(U32), t["qzt"][:].bitcast(U32), 4, 0x0F0F0F0F,
        ALU.logical_shift_right, ALU.bitwise_and), wld, "prep")
    zq_done = p.c("prep")
    for half, nscz in ((0, "nscz_g"), (1, "nscz_u")):
        for (t5, par, K, off) in TILES:
            src = t["zq8"][:, 1376 * par + 688 * half + 128 * t5:
                           1376 * par + 688 * half + 128 * t5 + K]
            dst = t[nscz][:, off:off + K]
            in1 = t["sc_sb"][:, 1376 * half + off:1376 * half + off + K]
            p.emit("vector", lambda e, s=src, d=dst, i1=in1: e.scalar_tensor_tensor(
                d, s, 1.0, i1, ALU.add, ALU.mult), [("prep", zq_done)], "prep")
    nscz_pos = p.c("prep")
    p.emit("vector", lambda e: e.tensor_scalar_mul(t["nscz_g"][:], t["nscz_g"][:], -1.0), [("prep", nscz_pos)], "prep")
    p.emit("vector", lambda e: e.tensor_scalar_mul(t["nscz_u"][:], t["nscz_u"][:], -1.0), [("prep", nscz_pos)], "prep")

    hts_seq = 0
    for bi, btiles in enumerate(BLOCKS):
        bo = btiles[0][3]
        bW = sum(K for (_, _, K, _) in btiles)
        deq_at_dg = []
        for dg in range(16):
            db = dg % 2
            deq_at_dg.append(p.c("deq"))
            if dg >= 2:
                wq = [("deq", deq_at_dg[dg - 1])]
            elif bi > 0:
                wq = [("deq", p.c("deq"))]
            else:
                wq = []
            qsem = "qb" + str(db)
            p.dma("sync", t["qbs"][db][:, 0:bW], t["qb_gu"][128 * dg:128 * dg + 128, bo:bo + bW], qsem, wq)
            p.dma("sync", t["qbs"][db][:, bW:2 * bW], t["qb_gu"][128 * dg:128 * dg + 128, 1376 + bo:1376 + bo + bW], qsem, wq)
            qb_c = p.c(qsem)
            wnib = [(qsem, qb_c)]
            if dg >= 2:
                wnib.append(("deq", deq_at_dg[dg - 1]))
            elif bi > 0:
                wnib.append(("deq", p.c("deq")))
            if bi > 0 and dg == 0:
                wnib.append(("gu", p.c("gu")))
            p.emit("vector", lambda e, bW=bW, db=db: e.tensor_single_scalar(
                t["nibs0"][db][:].bitcast(U32)[:, 0:bW // 2], t["qbs"][db][:].bitcast(U32)[:, 0:bW // 2],
                0x0F0F0F0F, ALU.bitwise_and), wnib, "nib")
            p.emit("vector", lambda e, bW=bW, db=db: e.tensor_scalar(
                t["nibs1"][db][:].bitcast(U32)[:, 0:bW // 2], t["qbs"][db][:].bitcast(U32)[:, 0:bW // 2],
                4, 0x0F0F0F0F, ALU.logical_shift_right, ALU.bitwise_and), wnib, "nib")
            nib_c = p.c("nib")
            for half in (0, 1):
                for (co, cw) in chunks(bW, 512):
                    bank = (half * 2 + (co // 512)) % 2
                    wsc = [("ld", ld_all), ("deq", max(0, p.c("deq") - 2))]
                    sc_rhs = t["sc_sb"][:, 1376 * half + bo + co: 1376 * half + bo + co + cw]
                    ind_l = t["indf"][:, 128 * (2 * dg):128 * (2 * dg) + 128]
                    p.emit("tensor", lambda e, o=t["psc"][bank], il=ind_l, r=sc_rhs, cw=cw:
                           e.matmul(o[:, 0:cw], il, r, start=True, stop=True), wsc, "sc")
                    sc_c = p.c("sc")
                    for par in (0, 1):
                        kk = 2 * dg + par
                        nib = t["nibs0"][db] if par == 0 else t["nibs1"][db]
                        dst = t["w_sb"][:, kk * 2 * bW + half * bW + co: kk * 2 * bW + half * bW + co + cw]
                        wmult = [("nib", nib_c), ("sc", sc_c)]
                        if bi > 0 and dg == 0:
                            wmult.append(("gu", p.c("gu")))
                        p.emit("vector", lambda e, d=dst, n=nib, co2=half * bW + co, cw=cw, b=bank:
                               e.tensor_tensor(d, n[:, co2:co2 + cw], t["psc"][b][:, 0:cw], ALU.mult),
                               wmult, "deq")
        deq_block = p.c("deq")

        for T in range(16):
            r, coff = (512 * T) // 1024, (512 * T) % 1024
            wx = [("cc1", 1)] + ([("gu", p.c("gu"))] if (T > 0 or bi > 0) else [])
            p.dma("sync", t["xt"][:].rearrange("p (k t) -> p k t", t=512),
                  t["ag1_out"][4096 * r:4096 * r + 4096, coff:coff + 512]
                  .rearrange("(k p) t -> p k t", p=128), "x", wx)
            x_c = p.c("x")
            wsg = [("x", x_c), ("sgc", p.c("sgc"))]
            for kk in range(32):
                p.emit("tensor", lambda e, kk=kk: e.matmul(
                    t["psg"][:], t["indb"][:, 32 * kk:32 * kk + 32], t["xt"][:, 512 * kk:512 * kk + 512],
                    start=(kk == 0), stop=(kk == 31)),
                    wsg if kk == 0 else [], "sg" if kk == 31 else None)
            sg_c = p.c("sg")
            p.emit("vector", lambda e: e.tensor_copy(t["sg_sb"][:], t["psg"][:]), [("sg", sg_c)], "sgc")
            sgc_c = p.c("sgc")

            for ji, (t5, par, K, off) in enumerate(btiles):
                pr = ji % 2
                pg, pu = t["pg"][pr], t["pu"][pr]
                loff = off - bo
                wg = [("deq", deq_block), ("sgc", sgc_c), ("sil", max(0, p.c("sil") - 1)), ("x", x_c)]
                for kk in range(32):
                    lhs = t["w_sb"][:, kk * 2 * bW + loff:kk * 2 * bW + loff + K]
                    p.emit("tensor", lambda e, l=lhs, kk=kk, pg=pg, K=K: e.matmul(
                        pg[0:K, :], l, t["xt"][:, 512 * kk:512 * kk + 512],
                        start=(kk == 0), stop=False), wg if kk == 0 else [], None)
                p.emit("tensor", lambda e, K=K, off=off, pg=pg: e.matmul(
                    pg[0:K, :], t["nscz_g"][:, off:off + K], t["sg_sb"][:],
                    start=False, stop=True), [], "gu")
                g_c = p.c("gu")
                wu = [("hT", max(0, p.c("hT") - 1))]
                for kk in range(32):
                    lhs = t["w_sb"][:, kk * 2 * bW + bW + loff:kk * 2 * bW + bW + loff + K]
                    p.emit("tensor", lambda e, l=lhs, kk=kk, pu=pu, K=K: e.matmul(
                        pu[0:K, :], l, t["xt"][:, 512 * kk:512 * kk + 512],
                        start=(kk == 0), stop=False), wu if kk == 0 else [], None)
                p.emit("tensor", lambda e, K=K, off=off, pu=pu: e.matmul(
                    pu[0:K, :], t["nscz_u"][:, off:off + K], t["sg_sb"][:],
                    start=False, stop=True), [], "gu")
                u_c = p.c("gu")
                sb_ = hts_seq % 2
                p.emit("scalar", lambda e, K=K, pg=pg, sb_=sb_: e.activation(
                    t["silu_t"][sb_][0:K, :], pg[0:K, :], AF.Silu),
                    [("gu", g_c), ("hT", max(0, p.c("hT") - 1))], "sil")
                sil_c = p.c("sil")
                hb = hts_seq % 2
                hsem = "hts" + str(hb)
                whT = [("sil", sil_c), ("gu", u_c), (hsem, p.c(hsem))]
                p.emit("vector", lambda e, K=K, hb=hb, pu=pu, sb_=sb_: e.tensor_tensor(
                    t["hT_sb"][hb][0:K, :], t["silu_t"][sb_][0:K, :], pu[0:K, :], ALU.mult),
                    whT, "hT")
                hT_c = p.c("hT")
                jtg = bi * 6 + ji
                p.dma("gpsimd", t["hT_dram"][128 * jtg:128 * jtg + K, 512 * T:512 * T + 512],
                      t["hT_sb"][hb][0:K, :], hsem, [("hT", hT_c)])
                hts_seq += 1
    return p.c("gu"), (p.c("hts0"), p.c("hts1"))


def emit_ffn_down(p, t, gu_all, hts_all):
    nc = p.nc
    for q in range(4):
        p.dma("gpsimd", t["qkv_bf"][1024 * q:1024 * q + 1024, :],
              t["qkv_wT"][1024 * q:1024 * q + 1024, :], "wcast")
    wph = [("gu", p.c("gu")), ("hT", p.c("hT")), ("sil", p.c("sil"))]
    p.dma("sync", t["scd"][:], t["sc_down"], "ld", wph)
    p.dma("sync", t["qzd"][:], t["qz_down"], "ld", wph)
    p.dma("sync", t["inddf"][:], t["ind_dn_f32"], "ld", wph)
    p.dma("sync", t["inddb"][:], t["ind_dn_b"], "ld", wph)
    ld_all = p.c("ld")
    wld = [("ld", ld_all)]
    p.emit("vector", lambda e: e.tensor_single_scalar(
        t["zqd8"][:, 0:2048].bitcast(U32), t["qzd"][:].bitcast(U32), 0x0F0F0F0F, ALU.bitwise_and), wld, "prep")
    p.emit("vector", lambda e: e.tensor_scalar(
        t["zqd8"][:, 2048:4096].bitcast(U32), t["qzd"][:].bitcast(U32), 4, 0x0F0F0F0F,
        ALU.logical_shift_right, ALU.bitwise_and), wld, "prep")
    zqd_done = p.c("prep")
    for par in (0, 1):
        src = t["zqd8"][:, 2048 * par:2048 * par + 2048]
        dst = t["nsczd"][:, par::2]
        in1 = t["scd"][:, par::2]
        p.emit("vector", lambda e, s=src, d=dst, i1=in1: e.scalar_tensor_tensor(
            d, s, 1.0, i1, ALU.add, ALU.mult), [("prep", zqd_done)], "prep")
    nsczd_pos = p.c("prep")
    p.emit("vector", lambda e: e.tensor_scalar_mul(t["nsczd"][:], t["nsczd"][:], -1.0), [("prep", nsczd_pos)], "prep")
    prep_all = p.c("prep")

    for t5 in range(6):
        K = 128 if t5 < 5 else 48
        wq = [("deqd", p.c("deqd"))] if t5 > 0 else [("gu", gu_all)]
        p.dma("sync", t["qbd_sb"][0:K, :], t["qb_down"][128 * t5:128 * t5 + K, :], "qbd", wq)
        qbd_c = p.c("qbd")
        p.emit("vector", lambda e, K=K: e.tensor_single_scalar(
            t["nibd0"][0:K, :].bitcast(U32), t["qbd_sb"][0:K, :].bitcast(U32),
            0x0F0F0F0F, ALU.bitwise_and), [("qbd", qbd_c)], "nibd")
        p.emit("vector", lambda e, K=K: e.tensor_scalar(
            t["nibd1"][0:K, :].bitcast(U32), t["qbd_sb"][0:K, :].bitcast(U32),
            4, 0x0F0F0F0F, ALU.logical_shift_right, ALU.bitwise_and), [("qbd", qbd_c)], "nibd")
        nibd_c = p.c("nibd")
        for par in (0, 1):
            kt = 2 * t5 + par
            for (co, cw) in chunks(4096, 512):
                bank = (co // 512) % 2
                wsc = [("ld", ld_all), ("deqd", p.c("deqd"))]
                p.emit("tensor", lambda e, b=bank, kt=kt, co=co, cw=cw, K=K: e.matmul(
                    t["psc"][b][0:K, 0:cw], t["inddf"][:, TILES[kt][3]:TILES[kt][3] + K],
                    t["scd"][:, co:co + cw], start=True, stop=True), wsc, "scd_")
                scd_c = p.c("scd_")
                nib = t["nibd0"] if par == 0 else t["nibd1"]
                p.emit("vector", lambda e, kt=kt, co=co, cw=cw, K=K, n=nib, b=bank:
                       e.tensor_tensor(t["wd_sb"][0:K, 4096 * kt + co:4096 * kt + co + cw],
                                       n[0:K, co:co + cw], t["psc"][b][0:K, 0:cw], ALU.mult),
                       [("nibd", nibd_c), ("scd_", scd_c)], "deqd")
    deqd_all = p.c("deqd")

    rs_done = []
    for k4 in range(4):
      for cp in range(16):
        ti = 8 * (cp // 2) + 2 * k4 + (cp % 2)
        hb = ti % 2
        whts = [("hts0", hts_all[0]), ("hts1", hts_all[1])]
        if ti == 0:
            wht = whts
        else:
            wht = whts + [("pd", max(0, p.c("pd") - 8))]
        hlsem = "htl" + str(hb)
        p.dma("sync", t["ht"][hb][:].rearrange("p (k t) -> p k t", t=128),
              t["hT_dram"][:, 128 * ti:128 * ti + 128].rearrange("(k p) t -> p k t", p=128),
              hlsem, wht)
        htl_c = p.c(hlsem)
        wsh = [(hlsem, htl_c), ("shc", p.c("shc"))]
        for kt in range(12):
            K = TILES[kt][2]
            p.emit("tensor", lambda e, kt=kt, K=K, hb=hb: e.matmul(
                t["psh"][:, 0:128], t["inddb"][0:K, 12 * kt:12 * kt + 12],
                t["ht"][hb][0:K, 128 * kt:128 * kt + 128],
                start=(kt == 0), stop=(kt == 11)), wsh if kt == 0 else [], "sh" if kt == 11 else None)
        sh_c = p.c("sh")
        p.emit("vector", lambda e: e.tensor_copy(t["sh_sb"][:], t["psh"][:, 0:128]), [("sh", sh_c)], "shc")
        shc_c = p.c("shc")
        rb = ti % 2
        for ho in range(8):
            bank = ho % 4
            pd = t["pd_"][bank]
            wd0 = [("deqd", deqd_all), ("prep", prep_all), ("shc", shc_c),
                   ("rsv", max(0, p.c("rsv") - 3))]
            for kt in range(12):
                K = TILES[kt][2]
                p.emit("tensor", lambda e, kt=kt, K=K, ho=ho, pd=pd, hb=hb: e.matmul(
                    pd[:], t["ht"][hb][0:K, 128 * kt:128 * kt + 128],
                    t["wd_sb"][0:K, 4096 * kt + 512 * ho:4096 * kt + 512 * ho + 512],
                    start=(kt == 0), stop=False), wd0 if kt == 0 else [], None)
            p.emit("tensor", lambda e, ho=ho, pd=pd: e.matmul(
                pd[:], t["sh_sb"][:], t["nsczd"][:, 512 * ho:512 * ho + 512],
                start=False, stop=True), [], "pd")
            pd_c = p.c("pd")
            rsem = "rss" + str(rb)
            wev = [("pd", pd_c), (rsem, p.c(rsem))]
            p.emit("vector", lambda e, rb=rb, ho=ho, pd=pd: e.tensor_copy(
                t["rs_sb"][rb][:, 512 * ho:512 * ho + 512], pd[:]), wev, "rsv")
            rsv_c = p.c("rsv")
            if ho == 7:
                rrow = 2048 * k4 + 256 * (ti // 8) + 128 * (ti % 2)
                p.dma("scalar", t["rs_in"][rrow:rrow + 128, :], t["rs_sb"][rb][:],
                      rsem, [("rsv", rsv_c)])
      p.emit("gpsimd", lambda e, k4=k4: e.collective_compute(
          "ReduceScatter", ALU.add, replica_groups=[list(range(NCORES))],
          ins=[t["rs_in"][2048 * k4:2048 * k4 + 2048, :]],
          outs=[t["rs_out"][256 * k4:256 * k4 + 256, :]]),
          [("rss0", p.c("rss0")), ("rss1", p.c("rss1"))], "cc2")


def emit_attn(p, t):
    nc = p.nc
    p.dma("sync", t["qdec_sb"][:], t["qdec"], "ald")
    p.dma("sync", t["kdec_sb"][:], t["kdec"], "ald")
    p.dma("sync", t["ddT_sb"][:], t["ddT"], "ald")
    ald_all = p.c("ald")
    ident_c = 16

    p.dma("sync", t["qkvw"][:].rearrange("p (k o) -> p k o", o=1536),
          t["qkv_bf"][:, :].rearrange("(k p) o -> p k o", p=128),
          "qwl", [("wcast", p.c("wcast"))])
    qwl_c = p.c("qwl")

    for b in range(NBATCH):
        for T in range(SEQ // 512):
            tok0 = 2048 * b + 512 * T
            r, coff = tok0 // 1024, tok0 % 1024
            ab = T % 2
            wx = [("cc1", 1), ("pqkv", p.c("pqkv"))]
            p.dma("sync", t["axt"][ab][:].rearrange("p (k t) -> p k t", t=512),
                  t["ag1_out"][4096 * r:4096 * r + 4096, coff:coff + 512]
                  .rearrange("(k p) t -> p k t", p=128), "ax", wx)
            ax_c = p.c("ax")
            for o in range(12):
                bank = o % 2
                wmm = [("ax", ax_c), ("qwl", qwl_c), ("sqkv", max(0, p.c("sqkv") - 1))]
                if o < 2:
                    wmm = wmm + [("am20", p.c("am20")), ("am21", p.c("am21"))]
                for k in range(32):
                    p.emit("tensor", lambda e, o=o, k=k, bank=bank, ab=ab: e.matmul(
                        t["pqkv"][bank][:], t["qkvw"][:, 1536 * k + 128 * o:1536 * k + 128 * o + 128],
                        t["axt"][ab][:, 512 * k:512 * k + 512],
                        start=(k == 0), stop=(k == 31)),
                        wmm if k == 0 else [], "pqkv" if k == 31 else None)
                mm_c = p.c("pqkv")
                p.emit("scalar", lambda e, o=o, T=T, bank=bank: e.activation(
                    t["qkvT"][:, 2048 * o + 512 * T:2048 * o + 512 * T + 512],
                    t["pqkv"][bank][:], AF.Silu), [("pqkv", mm_c)], "sqkv")
        sq_all = p.c("sqkv")

        for hpair in range(2):
          for i in range(NB):
            for z in range(2):
                hc = 2 * hpair + z
                o_q, o_k, o_v = 3 * hc, 3 * hc + 1, 3 * hc + 2
                zz = str(z)
                cb = 256 * i
                wqd = [("sqkv", sq_all), ("ald", ald_all), ("am3" + zz, p.c("am3" + zz))]
                p.emit("vector", lambda e, o_q=o_q, cb=cb, hc=hc, z=z: e.tensor_tensor(
                    t["qd_sb"][z][:], t["qkvT"][:, 2048 * o_q + cb:2048 * o_q + cb + 256],
                    t["qdec_sb"][:, 256 * hc:256 * hc + 256], ALU.mult), wqd, "aev" + zz)
                wtr = [("sqkv", sq_all), ("ld", ident_c), ("atrev" + zz, p.c("atrev" + zz))]
                for s2 in range(2):
                    p.emit("tensor", lambda e, o_k=o_k, cb=cb, s2=s2, z=z: e.transpose(
                        t["ptr"][z][:, 128 * s2:128 * s2 + 128],
                        t["qkvT"][:, 2048 * o_k + cb + 128 * s2:2048 * o_k + cb + 128 * s2 + 128],
                        t["ident_sb"][:]), wtr if s2 == 0 else [], None)
                    p.emit("tensor", lambda e, o_v=o_v, cb=cb, s2=s2, z=z: e.transpose(
                        t["ptr"][z][:, 256 + 128 * s2:256 + 128 * s2 + 128],
                        t["qkvT"][:, 2048 * o_v + cb + 128 * s2:2048 * o_v + cb + 128 * s2 + 128],
                        t["ident_sb"][:]), [], ("atr" + zz) if s2 == 1 else None)
                atr_c = p.c("atr" + zz)
                for s2 in range(2):
                    p.emit("vector", lambda e, s2=s2, hc=hc, z=z: e.tensor_scalar(
                        t["kk_sb"][z][:, 128 * s2:128 * s2 + 128], t["ptr"][z][:, 128 * s2:128 * s2 + 128],
                        t["kdec_sb"][:, 2 * hc + s2:2 * hc + s2 + 1], None, ALU.mult),
                        [("atr" + zz, atr_c)], "atrev" + zz)
                    p.emit("vector", lambda e, s2=s2, z=z: e.tensor_copy(
                        t["vv_sb"][z][:, 128 * s2:128 * s2 + 128],
                        t["ptr"][z][:, 256 + 128 * s2:256 + 128 * s2 + 128]),
                        [("atr" + zz, atr_c)], "atrev" + zz)
                atrev_c = p.c("atrev" + zz)
                wm1 = [("am2" + zz, p.c("am2" + zz))]
                for ms in range(2):
                    p.emit("tensor", lambda e, o_k=o_k, o_q=o_q, cb=cb, ms=ms, z=z: e.matmul(
                        t["ps_s"][z][:, 256 * ms:256 * ms + 256],
                        t["qkvT"][:, 2048 * o_k + cb + 128 * ms:2048 * o_k + cb + 128 * ms + 128],
                        t["qkvT"][:, 2048 * o_q + cb:2048 * o_q + cb + 256],
                        start=True, stop=True), wm1 if ms == 0 else [], ("am1" + zz) if ms == 1 else None)
                am1_c = p.c("am1" + zz)
                for ms in range(2):
                    p.emit("vector", lambda e, ms=ms, hc=hc, z=z: e.tensor_tensor(
                        t["scs_sb"][z][:, 256 * ms:256 * ms + 256], t["ps_s"][z][:, 256 * ms:256 * ms + 256],
                        t["ddT_sb"][:, 512 * hc + 256 * ms:512 * hc + 256 * ms + 256], ALU.mult),
                        [("am1" + zz, am1_c)], ("am2" + zz) if ms == 1 else None)
                am2_c = p.c("am2" + zz)
                wm3 = [("am2" + zz, am2_c), ("aev" + zz, p.c("aev" + zz)), ("atrev" + zz, atrev_c),
                       ("aoev" + zz, p.c("aoev" + zz))]
                first = True
                for ns in range(2):
                    for ms in range(2):
                        last0 = (i == 0 and ms == 1)
                        p.emit("tensor", lambda e, ns=ns, ms=ms, last0=last0, z=z: e.matmul(
                            t["ps_att"][z][:, 128 * ns:128 * ns + 128],
                            t["scs_sb"][z][:, 256 * ms + 128 * ns:256 * ms + 128 * ns + 128],
                            t["vv_sb"][z][:, 128 * ms:128 * ms + 128],
                            start=(ms == 0), stop=last0), wm3 if first else [],
                            ("am3" + zz) if (last0 and ns == 1) else None)
                        first = False
                    if i > 0:
                        p.emit("tensor", lambda e, ns=ns, z=z: e.matmul(
                            t["ps_att"][z][:, 128 * ns:128 * ns + 128],
                            t["qd_sb"][z][:, 128 * ns:128 * ns + 128], t["kv_bf"][z][:],
                            start=False, stop=True), [("akvc" + zz, p.c("akvc" + zz))],
                            ("am3" + zz) if ns == 1 else None)
                am3_c = p.c("am3" + zz)
                wm5 = [("akv" + zz, p.c("akv" + zz))]
                for ms in range(2):
                    p.emit("tensor", lambda e, ms=ms, z=z: e.matmul(
                        t["ps_kv"][z][:, 0:128], t["kk_sb"][z][:, 128 * ms:128 * ms + 128],
                        t["vv_sb"][z][:, 128 * ms:128 * ms + 128],
                        start=(ms == 0), stop=(ms == 1)), wm5 if ms == 0 else [],
                        ("am5" + zz) if ms == 1 else None)
                am5_c = p.c("am5" + zz)
                for ns in range(2):
                    p.emit("vector", lambda e, ns=ns, z=z: e.tensor_copy(
                        t["att_sb"][z][:, 128 * ns:128 * ns + 128], t["ps_att"][z][:, 128 * ns:128 * ns + 128]),
                        [("am3" + zz, am3_c), ("ast", p.c("ast"))], ("aoev" + zz) if ns == 1 else None)
                aoev_c = p.c("aoev" + zz)
                if i == 0:
                    p.emit("vector", lambda e, z=z: e.tensor_copy(t["kv_sb"][z][:], t["ps_kv"][z][:, 0:128]),
                           [("am5" + zz, am5_c)], "akv" + zz)
                else:
                    p.emit("vector", lambda e, hc=hc, z=z: e.scalar_tensor_tensor(
                        t["kv_sb"][z][:], t["kv_sb"][z][:], t["kdec_sb"][:, 8 + hc:8 + hc + 1],
                        t["ps_kv"][z][:, 0:128], ALU.mult, ALU.add),
                        [("am5" + zz, am5_c)], "akv" + zz)
                akv_c = p.c("akv" + zz)
                if i < NB - 1:
                    p.emit("vector", lambda e, z=z: e.tensor_copy(t["kv_bf"][z][:], t["kv_sb"][z][:]),
                           [("akv" + zz, akv_c)], "akvc" + zz)
                colb = 128 * hc
                p.dma("sync",
                      t["ag2_in"][2048 * b + cb:2048 * b + cb + 256, colb:colb + 128]
                      .rearrange("(s r) c -> r s c", r=128),
                      t["att_sb"][z][:].rearrange("p (s c) -> p s c", c=128),
                      "ast", [("aoev" + zz, aoev_c)])
        p.emit("gpsimd", lambda e, b=b: e.collective_compute(
            "AllGather", ALU.bypass, replica_groups=[list(range(NCORES))],
            ins=[t["ag2_in"][2048 * b:2048 * b + 2048, :]],
            outs=[t["ag2_out"][16384 * b:16384 * b + 16384, :]]),
            [("ast", p.c("ast"))], "cc3")
        if b == 0:
            for q in range(4):
                p.dma("gpsimd", t["gate_bf"][1024 * q:1024 * q + 1024, :],
                      t["gate_wT"][1024 * q:1024 * q + 1024, :], "wcast2")
                p.dma("gpsimd", t["out_bf"][1024 * q:1024 * q + 1024, :],
                      t["out_wT"][1024 * q:1024 * q + 1024, :], "wcast2")


def emit_tail(p, t):
    nc = p.nc
    p.dma("sync", t["cid_sb"][:], t["core_c"], "ld")
    p.dma("sync", t["aoff_sb"][:], t["ag_off"], "ld")
    p.dma("sync", t["nw_sb"][:], t["norm_w_t"], "ld")
    ld_all = p.c("ld")

    regs = {}

    def emit_dyn_dma(engname, outap, make_in, wait, inc):
        def fn(e):
            if engname not in regs:
                regs[engname] = nc.values_load(t["cid_sb"][0:1, 0:1], engines=[e.engine],
                                               min_val=0, max_val=7,
                                               skip_runtime_bounds_check=True)
            return e.dma_start(out=outap, in_=make_in(regs[engname]))
        return p.emit(engname, fn, wait, inc, 16)

    emit_dyn_dma("sync", t["xno_dram"][:, :],
                 lambda reg: t["ag1_out"][bass.DynSlice(reg * 4096, 4096), :],
                 [("cc1", 1), ("ld", ld_all)], "txl")
    p.dma("sync", t["xnoT"][:].rearrange("p (k t) -> p k t", t=1024),
          t["xno_dram"][:, :].rearrange("(k p) t -> p k t", p=128),
          "txl", [("txl", p.c("txl"))])
    txl_c = p.c("txl")

    for jt in range(32):
        gb = jt % 2
        gsem = "gwl" + str(gb)
        wgl = [("wcast2", p.c("wcast2")), ("gmm", max(0, p.c("gmm") - 1))]
        p.dma("sync", t["gw_t"][gb][:].rearrange("p (k o) -> p k o", o=128),
              t["gate_bf"][:, 128 * jt:128 * jt + 128].rearrange("(k p) o -> p k o", p=128),
              gsem, wgl)
        gwl_c = p.c(gsem)
        wmm = [(gsem, gwl_c), ("txl", txl_c), ("gsg", p.c("gsg"))]
        for th in range(2):
            for k in range(32):
                p.emit("tensor", lambda e, k=k, th=th, gb=gb: e.matmul(
                    t["pgate"][th][:], t["gw_t"][gb][:, 128 * k:128 * k + 128],
                    t["xnoT"][:, 1024 * k + 512 * th:1024 * k + 512 * th + 512],
                    start=(k == 0), stop=(k == 31)),
                    wmm if (k == 0 and th == 0) else [],
                    "gmm" if (k == 31 and th == 1) else None)
        gmm_c = p.c("gmm")
        for th in range(2):
            p.emit("scalar", lambda e, jt=jt, th=th: e.activation(
                t["pnT"][:, 1024 * jt + 512 * th:1024 * jt + 512 * th + 512],
                t["pgate"][th][:], AF.Sigmoid),
                [("gmm", gmm_c)], "gsg")
    gsg_all = p.c("gsg")

    regs2 = {}

    def emit_dyn2(engname, outap, make_in, wait, inc):
        def fn(e):
            if engname not in regs2:
                regs2[engname] = nc.values_load(t["aoff_sb"][0:1, 0:1], engines=[e.engine],
                                                min_val=0, max_val=50176,
                                                skip_runtime_bounds_check=True)
            return e.dma_start(out=outap, in_=make_in(regs2[engname]))
        return p.emit(engname, fn, wait, inc, 16)

    wex0 = [("cc3", 4), ("ld", ld_all)]
    for r in range(NCORES):
        emit_dyn2("sync", t["att_dram"][:, 512 * r:512 * r + 512],
                  lambda reg, r=r: t["ag2_out"][
                      bass.DynSlice(reg + 2048 * r, 1024), :],
                  wex0 if r == 0 else [], "tex0")
    tex0_c = p.c("tex0")

    for ti in range(8):
        ab = ti % 2
        wex = [("tex0", tex0_c), ("trm", p.c("trm")), ("tsc", p.c("tsc"))]
        p.dma("sync", t["att_t"][ab][:], t["att_dram"][128 * ti:128 * ti + 128, :], "tex", wex)
        tex_c = p.c("tex")
        p.emit("scalar", lambda e, ab=ab: e.activation(
            t["tmp_t"][:], t["att_t"][ab][:], AF.Square, accum_out=t["ssq"][:]),
            [("tex", tex_c), ("ttr", p.c("ttr"))], "tsc")
        p.emit("scalar", lambda e: e.activation(t["std_c"][:], t["ssq"][:], AF.Sqrt,
                                                bias=np.float32(1e-5), scale=np.float32(1.0 / HID)),
               [("tsc", p.c("tsc"))], "tsc")
        st_c = p.c("tsc")
        p.emit("vector", lambda e: e.reciprocal(t["rstd_c"][:], t["std_c"][:]), [("tsc", st_c)], "trm")
        r_c = p.c("trm")
        p.emit("vector", lambda e, ab=ab: e.tensor_scalar(
            t["tmp_t"][:], t["att_t"][ab][:], t["rstd_c"][:], None, ALU.mult),
            [("trm", r_c), ("ttr", p.c("ttr"))], "trm")
        tmp_c = p.c("trm")
        for grp in range(4):
            bank = grp % 2
            wtr = [("trm", tmp_c), ("tte", max(0, p.c("tte") - 1))]
            for j in range(8):
                jc = grp * 8 + j
                p.emit("tensor", lambda e, jc=jc, j=j, bank=bank: e.transpose(
                    t["ptr2"][bank][:, 128 * j:128 * j + 128],
                    t["tmp_t"][:, 128 * jc:128 * jc + 128],
                    t["ident_sb"][:]), wtr if j == 0 else [], "ttr" if j == 7 else None)
            tr_c = p.c("ttr")
            for j in range(8):
                jc = grp * 8 + j
                p.emit("vector", lambda e, jc=jc, j=j, ti=ti, bank=bank: e.scalar_tensor_tensor(
                    t["pnT"][:, 1024 * jc + 128 * ti:1024 * jc + 128 * ti + 128],
                    t["ptr2"][bank][:, 128 * j:128 * j + 128],
                    t["nw_sb"][:, jc:jc + 1],
                    t["pnT"][:, 1024 * jc + 128 * ti:1024 * jc + 128 * ti + 128],
                    ALU.mult, ALU.mult),
                    [("ttr", tr_c), ("gsg", gsg_all)], "tte" if j == 7 else None)
    norm_done = p.c("tte")

    for ho in range(32):
        ob = ho % 2
        osem = "owl" + str(ob)
        wol = [("omm", max(0, p.c("omm") - 1)), ("gmm", p.c("gmm") if ho < 2 else 0)]
        p.dma("sync", t["ow_t"][ob][:].rearrange("p (k o) -> p k o", o=128),
              t["out_bf"][:, 128 * ho:128 * ho + 128].rearrange("(k p) o -> p k o", p=128),
              osem, wol)
        owl_c = p.c(osem)
        wmm = [(osem, owl_c), ("tte", p.c("tte")), ("osv", p.c("osv"))]
        for th in range(2):
            for k in range(32):
                p.emit("tensor", lambda e, k=k, th=th, ob=ob: e.matmul(
                    t["pout"][th][:], t["ow_t"][ob][:, 128 * k:128 * k + 128],
                    t["pnT"][:, 1024 * k + 512 * th:1024 * k + 512 * th + 512],
                    start=(k == 0), stop=(k == 31)),
                    wmm if (k == 0 and th == 0) else [],
                    "omm" if (k == 31 and th == 1) else None)
        omm_c = p.c("omm")
        for th in range(2):
            p.emit("vector", lambda e, th=th: e.tensor_copy(
                t["sT_scr"][:, 512 * th:512 * th + 512], t["pout"][th][:]),
                [("omm", omm_c), ("ttr2", p.c("ttr2"))], "osv")
        osv_c = p.c("osv")
        bank2 = ho % 2
        wtr = [("osv", osv_c), ("tte2", max(0, p.c("tte2") - 1))]
        for j in range(8):
            p.emit("tensor", lambda e, j=j, bank2=bank2: e.transpose(
                t["ptr2"][bank2][:, 128 * j:128 * j + 128],
                t["sT_scr"][:, 128 * j:128 * j + 128],
                t["ident_sb"][:]), wtr if j == 0 else [], "ttr2" if j == 7 else None)
        tr_c = p.c("ttr2")
        p.emit("vector", lambda e, ho=ho, bank2=bank2: e.tensor_copy(
            t["s_buf"][:].rearrange("p (tt c) -> p tt c", c=4096)[:, :, 128 * ho:128 * ho + 128],
            t["ptr2"][bank2][:].rearrange("p (j c) -> p j c", c=128)),
            [("ttr2", tr_c)], "tte2")
    s_done = p.c("tte2")

    for t2 in range(16):
        ti, hh = t2 // 2, t2 % 2
        fb = t2 % 2
        wfl = [("tte2", s_done), ("cc2", 4), ("fin", max(0, p.c("fin") - 2)), ("tsc2", p.c("tsc2"))]
        xsem = "txf" + str(fb)
        p.dma("sync", t["d_t"][fb][:],
              t["rs_out"][128 * ti:128 * ti + 128, 2048 * hh:2048 * hh + 2048], "tdl", wfl)
        p.dma("sync", t["xf_t"][fb][:],
              t["x_own"][128 * ti:128 * ti + 128, 2048 * hh:2048 * hh + 2048], xsem,
              [("fin", max(0, p.c("fin") - 2))])
        tdl_c = p.c("tdl")
        xf_c = p.c(xsem)
        p.emit("scalar", lambda e, fb=fb: e.activation(
            t["g_t"][:], t["d_t"][fb][:], AF.Silu),
            [("tdl", tdl_c), (xsem, xf_c), ("fin", max(0, p.c("fin") - 1))], "tsc2")
        g_c = p.c("tsc2")
        p.emit("vector", lambda e, ti=ti, hh=hh: e.tensor_tensor(
            t["sg_t"][:], t["s_buf"][:, 4096 * ti + 2048 * hh:4096 * ti + 2048 * hh + 2048],
            t["g_t"][:], ALU.mult), [("tsc2", g_c), ("fst", p.c("fst"))], "fin")
        p.emit("vector", lambda e, fb=fb: e.tensor_add(
            t["sg_t"][:], t["sg_t"][:], t["xf_t"][fb][:]), [("fin", p.c("fin"))], "fin")
        fin_c = p.c("fin")
        p.dma("sync", t["out"][128 * ti:128 * ti + 128, 2048 * hh:2048 * hh + 2048],
              t["sg_t"][:], "fst", [("fin", fin_c)])
    p.emit("sync", lambda e: e.nop(), [("fst", p.c("fst"))], None)


SEMS = ("ld", "xld", "p0s", "p0v", "p0t", "p0e", "agi", "cc1",
        "prep", "qb0", "qb1", "nib", "sc", "deq", "x", "sg", "sgc", "gu", "sil", "hT", "hts0", "hts1",
        "qbd", "nibd", "scd_", "deqd", "htl0", "htl1", "sh", "shc", "pd", "rsv", "rss0", "rss1", "cc2",
        "ald", "qwl", "ax", "pqkv", "sqkv", "aev0", "aev1", "atr0", "atr1", "atrev0", "atrev1",
        "am10", "am11", "am20", "am21", "am30", "am31", "am50", "am51",
        "aoev0", "aoev1", "akv0", "akv1", "akvc0", "akvc1", "ast", "cc3",
        "txl", "tex", "tsc", "trm", "ttr", "tte", "gwl0", "gwl1", "gmm", "gsg", "tpt",
        "owl0", "owl1", "omm", "osv", "ttr2", "tte2", "tdl", "txf0", "txf1", "tsc2", "fin", "fst", "tex0", "wcast", "wcast2")

NO_GATE = ("cc1", "cc2", "cc3")


def build_nc():
    nc = bass.Bass("TRN2", target_bir_lowering=False)
    t = {}
    EI = lambda n, sh, dt: t.__setitem__(n, nc.dram_tensor(n, sh, dt, kind="ExternalInput").ap())
    EI("x_own", [TOK_PER, HID], F32)
    EI("qkv_wT", [HID, 1536], F32)
    EI("gate_wT", [HID, HID], F32)
    EI("out_wT", [HID, HID], F32)
    EI("norm_w_t", [128, 32], F32)
    EI("qb_gu", [2048, 2752], U8)
    EI("sc_gu", [32, 2752], F32)
    EI("qz_gu", [32, 344], U32)
    EI("ind_hid_f32", [32, HID], F32)
    EI("ind_hid_b", [128, 1024], BF16)
    EI("qb_down", [BETA_PER, 4096], U8)
    EI("sc_down", [12, 4096], F32)
    EI("qz_down", [12, 512], U32)
    EI("ind_dn_f32", [12, F_PER], F32)
    EI("ind_dn_b", [128, 144], BF16)
    EI("ident", [128, 128], BF16)
    EI("qdec", [128, 1024], BF16)
    EI("kdec", [128, 12], F32)
    EI("ddT", [128, 2048], BF16)
    EI("core_c", [1, 1], I32)
    EI("ag_off", [1, 1], I32)
    t["out"] = nc.dram_tensor("out", [TOK_PER, HID], F32, kind="ExternalOutput").ap()

    t["ag1_in_h"] = nc.dram_tensor("ag1_in", [HID, TOK_PER], BF16, kind="Internal")
    t["ag1_out_h"] = nc.dram_tensor("ag1_out", [NCORES * HID, TOK_PER], BF16, kind="Internal")
    t["ag2_in_h"] = nc.dram_tensor("ag2_in", [TOK, 512], BF16, kind="Internal")
    t["ag2_out_h"] = nc.dram_tensor("ag2_out", [NCORES * TOK, 512], BF16, kind="Internal")
    t["rs_in_h"] = nc.dram_tensor("rs_in", [TOK, HID], BF16, kind="Internal")
    t["rs_out_h"] = nc.dram_tensor("rs_out", [TOK_PER, HID], BF16, kind="Internal")
    t["hT_dram_h"] = nc.dram_tensor("hT_dram", [1536, TOK], BF16, kind="Internal")
    t["xno_dram_h"] = nc.dram_tensor("xno_dram", [HID, TOK_PER], BF16, kind="Internal")
    t["qkv_bf_h"] = nc.dram_tensor("qkv_bf", [HID, 1536], BF16, kind="Internal")
    t["gate_bf_h"] = nc.dram_tensor("gate_bf", [HID, HID], BF16, kind="Internal")
    t["out_bf_h"] = nc.dram_tensor("out_bf", [HID, HID], BF16, kind="Internal")
    t["att_dram_h"] = nc.dram_tensor("att_dram", [TOK_PER, HID], BF16, kind="Internal")
    for n in ("ag1_in", "ag1_out", "ag2_in", "ag2_out", "rs_in", "rs_out", "hT_dram",
              "xno_dram", "att_dram", "qkv_bf", "gate_bf", "out_bf"):
        t[n] = t[n + "_h"].ap()

    stack = contextlib.ExitStack()
    S = lambda st, name, shape, dt: st.enter_context(nc.sbuf_tensor(name, shape, dt))
    P = lambda st, name, shape, dt=F32: st.enter_context(nc.psum_tensor(name, shape, dt))

    t["ident_sb"] = S(stack, "ident_sb", [128, 128], BF16)
    t["cid_sb"] = S(stack, "cid_sb", [1, 1], I32)
    t["aoff_sb"] = S(stack, "aoff_sb", [1, 1], I32)
    t["nw_sb"] = S(stack, "nw_sb", [128, 32], F32)
    for n in ("ssq", "mean_c", "msq_c", "var_c", "std_c", "rstd_c", "nmr_c"):
        t[n] = S(stack, n, [128, 1], F32)

    p = Prog(nc)
    with nc.Block() as block:
        for s in SEMS:
            p.sem(stack, s)

        def gate():
            p.set_gate([(s, p.c(s)) for s in SEMS if s not in NO_GATE and p.c(s) > 0])

        with contextlib.ExitStack() as st0:
            t["x_sb"] = [S(st0, "x_sb0", [128, HID], F32), S(st0, "x_sb1", [128, HID], F32)]
            t["sq_scr"] = S(st0, "sq_scr", [128, HID], F32)
            t["xn_sb"] = S(st0, "xn_sb", [128, HID], BF16)
            t["stg_sb"] = S(st0, "stg_sb", [128, HID], BF16)
            t["ptr0"] = [P(st0, "ptr00", [128, 1024], BF16), P(st0, "ptr01", [128, 1024], BF16)]
            emit_phase0(p, t)

        with contextlib.ExitStack() as stf:
            t["w_sb"] = S(stf, "w_sb", [128, 49152], BF16)
            _qq = S(stf, "qq", [128, 4096], U8)
            _nibA = S(stf, "nibA", [128, 4096], U8)
            _nibB = S(stf, "nibB", [128, 4096], U8)
            _qq2 = S(stf, "qq2", [128, 1536], U8)
            _nibA2 = S(stf, "nibA2", [128, 1536], U8)
            _nibB2 = S(stf, "nibB2", [128, 1536], U8)
            t["qbs"] = [_qq, _qq2]
            t["nibs0"] = [_nibA, _nibA2]
            t["nibs1"] = [_nibB, _nibB2]
            t["qbd_sb"], t["nibd0"], t["nibd1"] = _qq, _nibA, _nibB
            t["wd_sb"] = t["w_sb"]
            t["hT_sb"] = [S(stf, "hT_sb0", [128, 512], BF16), S(stf, "hT_sb1", [128, 512], BF16)]
            t["psc"] = [P(stf, "psc0", [128, 512]), P(stf, "psc1", [128, 512])]
            t["psg"] = P(stf, "psg", [32, 512])
            t["pg"] = [P(stf, "pg0", [128, 512]), P(stf, "pg1", [128, 512])]
            t["pu"] = [P(stf, "pu0", [128, 512]), P(stf, "pu1", [128, 512])]
            t["psh"] = P(stf, "psh", [12, 512])
            t["pd_"] = [t["pg"][0], t["pg"][1], t["pu"][0], t["pu"][1]]
            gate()
            with contextlib.ExitStack() as stg:
                t["sc_sb"] = S(stg, "sc_sb", [32, 2752], F32)
                t["qzt"] = S(stg, "qzt", [32, 344], U32)
                t["zq8"] = S(stg, "zq8", [32, 2752], U8)
                t["nscz_g"] = S(stg, "nscz_g", [32, F_PER], BF16)
                t["nscz_u"] = S(stg, "nscz_u", [32, F_PER], BF16)
                t["indf"] = S(stg, "indf", [32, HID], F32)
                t["indb"] = S(stg, "indb", [128, 1024], BF16)
                t["xt"] = S(stg, "xt", [128, 32 * 512], BF16)
                t["sg_sb"] = S(stg, "sg_sb", [32, 512], BF16)
                t["silu_t"] = [S(stg, "silu_t0", [128, 512], BF16), S(stg, "silu_t1", [128, 512], BF16)]
                gu_all, hts_all = emit_ffn_gu(p, t)
            with contextlib.ExitStack() as std_:
                t["scd"] = S(std_, "scd", [12, 4096], F32)
                t["qzd"] = S(std_, "qzd", [12, 512], U32)
                t["zqd8"] = S(std_, "zqd8", [12, 4096], U8)
                t["nsczd"] = S(std_, "nsczd", [12, 4096], BF16)
                t["inddf"] = S(std_, "inddf", [12, F_PER], F32)
                t["inddb"] = S(std_, "inddb", [128, 144], BF16)
                t["ht"] = [S(std_, "ht0", [128, 12 * 128], BF16),
                           S(std_, "ht1", [128, 12 * 128], BF16)]
                t["sh_sb"] = S(std_, "sh_sb", [12, 128], BF16)
                t["rs_sb"] = [S(std_, "rs_sb0", [128, HID], BF16), S(std_, "rs_sb1", [128, HID], BF16)]
                emit_ffn_down(p, t, gu_all, hts_all)

        with contextlib.ExitStack() as sta:
            t["qdec_sb"] = S(sta, "qdec_sb", [128, 1024], BF16)
            t["kdec_sb"] = S(sta, "kdec_sb", [128, 12], F32)
            t["ddT_sb"] = S(sta, "ddT_sb", [128, 2048], BF16)
            t["qkvw"] = S(sta, "qkvw", [128, 32 * 1536], BF16)
            _ax = S(sta, "axt0", [128, 32 * 512], BF16)
            t["axt"] = [_ax, _ax]
            t["qkvT"] = S(sta, "qkvT", [128, 12 * 2048], BF16)
            t["qd_sb"] = [S(sta, "qd_sb0", [128, 256], BF16), S(sta, "qd_sb1", [128, 256], BF16)]
            t["kk_sb"] = [S(sta, "kk_sb0", [128, 256], BF16), S(sta, "kk_sb1", [128, 256], BF16)]
            t["vv_sb"] = [S(sta, "vv_sb0", [128, 256], BF16), S(sta, "vv_sb1", [128, 256], BF16)]
            t["scs_sb"] = [S(sta, "scs_sb0", [128, 512], BF16), S(sta, "scs_sb1", [128, 512], BF16)]
            t["att_sb"] = [S(sta, "att_sb0", [128, 256], BF16), S(sta, "att_sb1", [128, 256], BF16)]
            t["kv_sb"] = [S(sta, "kv_sb0", [128, 128], F32), S(sta, "kv_sb1", [128, 128], F32)]
            t["kv_bf"] = [S(sta, "kv_bf0", [128, 128], BF16), S(sta, "kv_bf1", [128, 128], BF16)]
            _ps0 = P(sta, "pqkv0", [128, 512])
            _ps1 = P(sta, "pqkv1", [128, 512])
            t["pqkv"] = [_ps0, _ps1]
            t["ps_s"] = [_ps0, _ps1]
            t["ptr"] = [P(sta, "ptr0", [128, 1024], BF16), P(sta, "ptr1", [128, 1024], BF16)]
            t["ps_att"] = [P(sta, "ps_att0", [128, 512]), P(sta, "ps_att1", [128, 512])]
            t["ps_kv"] = [P(sta, "ps_kv0", [128, 512]), P(sta, "ps_kv1", [128, 512])]
            gate()
            emit_attn(p, t)

        with contextlib.ExitStack() as stt:
            t["xnoT"] = S(stt, "xnoT", [128, 32 * 1024], BF16)
            t["s_buf"] = t["xnoT"]
            t["pnT"] = S(stt, "pnT", [128, 32 * 1024], BF16)
            _at = S(stt, "att_t0", [128, HID], BF16)
            t["att_t"] = [_at, _at]
            t["tmp_t"] = S(stt, "tmp_t", [128, HID], BF16)
            t["gsig"] = S(stt, "gsig", [128, 1024], BF16)
            _gw0 = S(stt, "gw_t0", [128, 32 * 128], BF16)
            _gw1 = S(stt, "gw_t1", [128, 32 * 128], BF16)
            t["gw_t"] = [_gw0, _gw1]
            t["ow_t"] = [_gw0, _gw1]
            t["sT_scr"] = S(stt, "sT_scr", [128, 1024], BF16)
            _dt = S(stt, "d_t0", [128, 2048], BF16)
            t["d_t"] = [_dt, _dt]
            t["xf_t"] = [S(stt, "x_sb2", [128, 2048], F32), S(stt, "x_sb3", [128, 2048], F32)]
            t["g_t"] = S(stt, "g_t", [128, 2048], BF16)
            t["sg_t"] = S(stt, "sg_t", [128, 2048], F32)
            t["pgate"] = [P(stt, "pgate0", [128, 512]), P(stt, "pgate1", [128, 512])]
            t["pout"] = [P(stt, "pout0", [128, 512]), P(stt, "pout1", [128, 512])]
            t["ptr2"] = [P(stt, "ptr20", [128, 1024], BF16), P(stt, "ptr21", [128, 1024], BF16)]
            gate()
            emit_tail(p, t)

        p.replay(block)
    stack.close()
    return nc


def _host_prep(core, inputs):
    c = core
    qw_gu = np.asarray(inputs["L0_ffn_qweight_gu"])
    qz_gu = np.asarray(inputs["L0_ffn_qzeros_gu"])
    sc_gu = np.asarray(inputs["L0_ffn_scales_gu"])
    gidx_gu = np.asarray(inputs["L0_ffn_g_idx_gu"])
    qw_d = np.asarray(inputs["L0_ffn_qweight_down"])
    qz_d = np.asarray(inputs["L0_ffn_qzeros_down"])
    sc_d = np.asarray(inputs["L0_ffn_scales_down"])
    gidx_d = np.asarray(inputs["L0_ffn_g_idx_down"])

    qb_gu_full = _host_prep.qb_gu_full
    qb_d_full = _host_prep.qb_d_full
    cols_nat = np.concatenate([np.arange(F_PER * c, F_PER * (c + 1)),
                               INTER + np.arange(F_PER * c, F_PER * (c + 1))])
    perm_f = np.array([f_local(t5, par, pp) for (t5, par, K, off) in TILES for pp in range(K)])
    cols_sig = np.concatenate([cols_nat[:F_PER][perm_f], cols_nat[F_PER:][perm_f]])
    qb_gu_c = np.ascontiguousarray(qb_gu_full[:, cols_sig])
    sc_gu_c = np.ascontiguousarray(sc_gu[:, cols_sig])
    w0 = (F_PER * c) // 8
    qz_gu_c = np.ascontiguousarray(
        np.concatenate([qz_gu[:, w0:w0 + F_PER // 8],
                        qz_gu[:, INTER // 8 + w0:INTER // 8 + w0 + F_PER // 8]], axis=1))
    qb_d_c = np.ascontiguousarray(qb_d_full[BETA_PER * c:BETA_PER * (c + 1), :])
    g0 = (F_PER * c) // GROUP
    ng = min(g0 + 12, N_DG) - g0
    sc_d_c = np.zeros((12, 4096), np.float32)
    sc_d_c[:ng] = sc_d[g0:g0 + ng]
    qz_d_c = np.zeros((12, 512), np.uint32)
    qz_d_c[:ng] = qz_d[g0:g0 + ng]

    ind_dn_f32 = np.zeros((12, F_PER), np.float32)
    ind_dn_b = np.zeros((128, 144), ml_dtypes.bfloat16)
    for kt, (t5, par, K, off) in enumerate(TILES):
        for pp in range(K):
            g = gidx_d[F_PER * c + f_local(t5, par, pp)] - g0
            if 0 <= g < 12:
                ind_dn_f32[g, off + pp] = 1.0
                ind_dn_b[pp, 12 * kt + g] = 1.0

    qdec = np.zeros((128, 1024), ml_dtypes.bfloat16)
    kdec = np.zeros((128, 12), np.float32)
    ddT = np.zeros((128, 2048), ml_dtypes.bfloat16)
    for hc in range(4):
        qd, kd, dd, bd = decay_tables(4 * c + hc)
        qdec[:, 256 * hc:256 * (hc + 1)] = np.broadcast_to(qd[None, :], (128, 256)).astype(ml_dtypes.bfloat16)
        kdec[:, 2 * hc] = kd[:128]
        kdec[:, 2 * hc + 1] = kd[128:]
        kdec[:, 8 + hc] = bd
        for ms in range(2):
            ddT[:, 512 * hc + 256 * ms:512 * hc + 256 * (ms + 1)] = \
                dd[128 * ms:128 * (ms + 1), :].astype(ml_dtypes.bfloat16)

    qkv_cols = np.arange(1536 * c, 1536 * (c + 1))
    qkv_wT_c = np.ascontiguousarray(_host_prep.qkv_wT_perm[:, qkv_cols])

    return dict(
        x_own=_host_prep.x_flat[TOK_PER * c:TOK_PER * (c + 1)],
        qkv_wT=qkv_wT_c,
        gate_wT=_host_prep.gate_wT_perm,
        out_wT=_host_prep.out_wT,
        norm_w_t=_host_prep.norm_w_t,
        qb_gu=qb_gu_c, sc_gu=sc_gu_c, qz_gu=qz_gu_c,
        ind_hid_f32=_host_prep.ind_hid_f32, ind_hid_b=_host_prep.ind_hid_b,
        qb_down=qb_d_c, sc_down=sc_d_c, qz_down=qz_d_c,
        ind_dn_f32=ind_dn_f32, ind_dn_b=ind_dn_b,
        ident=_host_prep.ident, qdec=qdec, kdec=kdec, ddT=ddT,
        core_c=np.array([[c]], np.int32),
        ag_off=np.array([[16384 * (c // 2) + 1024 * (c % 2)]], np.int32),
    )


def kernel(**inputs):
    x = np.asarray(inputs["L0_attn_hidden_states"])
    _host_prep.x_flat = np.ascontiguousarray(x.reshape(TOK, HID))
    qw_gu = np.asarray(inputs["L0_ffn_qweight_gu"])
    _host_prep.qb_gu_full = np.ascontiguousarray(
        qw_gu.view(np.uint8).reshape(512, 2 * INTER, 4).transpose(0, 2, 1).reshape(2048, 2 * INTER))
    qw_d = np.asarray(inputs["L0_ffn_qweight_down"])
    _host_prep.qb_d_full = np.ascontiguousarray(
        qw_d.view(np.uint8).reshape(F_PER, HID, 4).transpose(0, 2, 1).reshape(2 * INTER // 4, HID))
    _host_prep.qkv_wT_perm = np.ascontiguousarray(
        np.asarray(inputs["L0_attn_qkv_w"]).T[PERM_HID, :])
    _host_prep.gate_wT_perm = np.ascontiguousarray(
        np.asarray(inputs["L0_attn_gate_w"]).T[PERM_HID, :])
    _host_prep.out_wT = np.ascontiguousarray(np.asarray(inputs["L0_attn_out_w"]).T)
    _host_prep.norm_w_t = np.ascontiguousarray(
        np.asarray(inputs["L0_attn_norm_w"]).reshape(32, 128).T.astype(np.float32))
    gidx_gu = np.asarray(inputs["L0_ffn_g_idx_gu"])
    ind_hid_f32 = np.zeros((32, HID), np.float32)
    ind_hid_b = np.zeros((128, 1024), ml_dtypes.bfloat16)
    for kk in range(32):
        for pp in range(128):
            i_nat = PERM_HID[128 * kk + pp]
            g = gidx_gu[i_nat]
            ind_hid_f32[g, 128 * kk + pp] = 1.0
            ind_hid_b[pp, 32 * kk + g] = 1.0
    _host_prep.ind_hid_f32 = ind_hid_f32
    _host_prep.ind_hid_b = ind_hid_b
    _host_prep.ident = np.eye(128, dtype=ml_dtypes.bfloat16)

    nc = build_nc()
    in_maps = [_host_prep(c, inputs) for c in range(NCORES)]
    res = run_bass_kernel_spmd(nc, in_maps, core_ids=list(range(NCORES)))
    out = np.concatenate([res.results[c]["out"] for c in range(NCORES)], axis=0)
    return np.ascontiguousarray(out.reshape(NBATCH, SEQ, HID)).astype(np.float32)
